# version 13
# baseline (speedup 1.0000x reference)
import contextlib
import numpy as np
import ml_dtypes
import concourse.bass as bass
import concourse.mybir as mybir
from concourse.bass_utils import run_bass_kernel_spmd

F32 = mybir.dt.float32
BF16 = mybir.dt.bfloat16
I32 = mybir.dt.int32
AF = mybir.ActivationFunctionType
ALU = mybir.AluOpType
f32 = np.float32
BF = ml_dtypes.bfloat16

T = 2048
TH = 1024
D = 4096
NF = 32
NH = 16
EPS = 1e-5
EXP_SHIFT = -12.0
N_CORES = 8
DEBUG_SPILLS = False


def _build_tables():
    g = [1.0, 1.0, 1.0]
    dim, n = 3, 8
    masks = sorted(range(n), key=lambda m: (bin(m).count('1'), m))
    inv = {m: i for i, m in enumerate(masks)}
    C = np.zeros((n, n, n), dtype=np.float32)
    for i, a in enumerate(masks):
        for j, b in enumerate(masks):
            sw, at = 0, a >> 1
            while at:
                sw += bin(at & b).count('1')
                at >>= 1
            s = -1.0 if sw & 1 else 1.0
            for t in range(dim):
                if a >> t & 1 and b >> t & 1:
                    s *= float(g[t])
            C[i, j, inv[a ^ b]] = s
    grades = np.array([bin(m).count('1') for m in masks])
    rev = np.where(grades * (grades - 1) // 2 % 2 == 1, -1.0, 1.0).astype(np.float32)
    met = np.ones(8, dtype=np.float32)
    return C[:, :, [4, 5, 6]], rev, met


def _host_consts():
    biv, rev, met = _build_tables()
    scale = 1.0 / 16.0
    c = {}
    c["bivk3"] = np.ascontiguousarray(biv.transpose(2, 0, 1).reshape(3, 64)).astype(np.float32)
    c["rev8"] = rev.reshape(8, 1).astype(np.float32)
    c["d0t"] = np.ascontiguousarray(np.tile(scale * np.diag(rev * met), (1, 16))).astype(np.float32)
    p8 = np.zeros((8, 128), dtype=np.float32)
    for d_ in range(16):
        for i in range(8):
            p8[i, d_ * 8 + i] = 1.0
    c["p8"] = p8
    bdm = np.zeros((128, 128), dtype=np.float32)
    for d_ in range(16):
        bdm[d_ * 8:(d_ + 1) * 8, d_ * 8:(d_ + 1) * 8] = 1.0
    c["bdmask"] = bdm
    p16 = np.zeros((32, 128), dtype=np.float32)
    for e in range(16):
        for r in range(8):
            p16[e, e * 8 + r] = 1.0
    c["p16"] = p16
    p4e = np.zeros((32, 32), dtype=np.float32)
    for e in range(4):
        for r in range(8):
            p4e[e, e * 8 + r] = 1.0
    c["p4e"] = p4e
    dg8 = np.zeros((128, 32), dtype=np.float32)
    for e in range(16):
        for rt in range(8):
            for ep in range(4):
                dg8[e * 8 + rt, ep * 8 + rt] = 1.0
    c["diag8"] = dg8
    c["iota16"] = np.arange(32, dtype=np.float32).reshape(32, 1)
    return c


def _tile_w(Wt):
    return np.ascontiguousarray(
        Wt.reshape(NF, 128, 8, 512).transpose(2, 1, 0, 3).reshape(8, 128, NF * 512))


class Ledger:
    def __init__(self, es, nc):
        self.es, self.nc, self.d = es, nc, {}

    def s(self, name):
        if name not in self.d:
            self.d[name] = [self.nc.alloc_semaphore(name=name), 0]
        return self.d[name][0]

    def take(self, prefixes):
        out = []
        for name in list(self.d):
            if any(name.startswith(p) or name == p for p in prefixes):
                out.append((name, self.d[name][0], self.d[name][1]))
                del self.d[name]
        return out

    def inc(self, ins, name, amt):
        h = self.s(name)
        ins.then_inc(h, amt)
        self.d[name][1] += amt
        return self.d[name][1]

    def n(self, name):
        self.s(name)
        return self.d[name][1]


def build_program():
    nc = bass.Bass("TRN2", target_bir_lowering=False, detect_race_conditions=False)

    xT = nc.dram_tensor("xT", [D, T], F32, kind="ExternalInput").ap()
    wq = nc.dram_tensor("wq", [8, 128, NF * 512], F32, kind="ExternalInput").ap()
    wk = nc.dram_tensor("wk", [8, 128, NF * 512], F32, kind="ExternalInput").ap()
    wv = nc.dram_tensor("wv", [8, 128, NF * 512], F32, kind="ExternalInput").ap()
    wbs = nc.dram_tensor("wbs", [8, 128, NF * 512], F32, kind="ExternalInput").ap()
    bq_t = nc.dram_tensor("bq_t", [128, NF], F32, kind="ExternalInput").ap()
    bk_t = nc.dram_tensor("bk_t", [128, NF], F32, kind="ExternalInput").ap()
    bb_t = nc.dram_tensor("bb_t", [128, NF], F32, kind="ExternalInput").ap()
    bv_r = nc.dram_tensor("bv_r", [1, D], F32, kind="ExternalInput").ap()
    bwT = nc.dram_tensor("bwT", [3, 16], F32, kind="ExternalInput").ap()
    active = nc.dram_tensor("active", [1, 4], I32, kind="ExternalInput").ap()
    prior = nc.dram_tensor("prior", [32, 1], F32, kind="ExternalInput").ap()
    rb_in = nc.dram_tensor("rb_in", [32, 1], F32, kind="ExternalInput").ap()
    routW = nc.dram_tensor("routW", [32, D], F32, kind="ExternalInput").ap()
    loraA = nc.dram_tensor("loraA", [128, D], F32, kind="ExternalInput").ap()
    loraB = nc.dram_tensor("loraB", [128, D], F32, kind="ExternalInput").ap()
    bivk3 = nc.dram_tensor("bivk3", [3, 64], F32, kind="ExternalInput").ap()
    rev8 = nc.dram_tensor("rev8", [8, 1], F32, kind="ExternalInput").ap()
    d0t = nc.dram_tensor("d0t", [8, 128], F32, kind="ExternalInput").ap()
    p8 = nc.dram_tensor("p8", [8, 128], F32, kind="ExternalInput").ap()
    bdmask = nc.dram_tensor("bdmask", [128, 128], F32, kind="ExternalInput").ap()
    p16 = nc.dram_tensor("p16", [32, 128], F32, kind="ExternalInput").ap()
    p4e = nc.dram_tensor("p4e", [32, 32], F32, kind="ExternalInput").ap()
    diag8 = nc.dram_tensor("diag8", [128, 32], F32, kind="ExternalInput").ap()
    iota16 = nc.dram_tensor("iota16", [32, 1], F32, kind="ExternalInput").ap()

    outT = nc.dram_tensor("outT", [NF, 128, T], F32, kind="ExternalOutput").ap()

    spk = dict(kind="ExternalOutput") if DEBUG_SPILLS else {}
    qsp = nc.dram_tensor("qsp", [NF, 128, T], BF16, **spk).ap()
    ksp = nc.dram_tensor("ksp", [NF, 128, T], BF16, **spk).ap()
    vsp = nc.dram_tensor("vsp", [NH, 128, 4096], BF16, **spk).ap()
    aosp = nc.dram_tensor("aosp", [NF, 128, T], F32, **spk).ap()
    x2sp = nc.dram_tensor("x2sp", [NF, 128, T], F32, **spk).ap()
    ffsp = nc.dram_tensor("ffsp", [NF, 128, T], F32, **spk).ap()
    wk64 = nc.dram_tensor("wk64", [16, 64], F32).ap()
    if DEBUG_SPILLS:
        dbg_zw = nc.dram_tensor("dbg_zw", [32, T], BF16, kind="ExternalOutput").ap()
        dbg_ln = nc.dram_tensor("dbg_ln", [128, NF * TH], BF16, kind="ExternalOutput").ap()
        dbg_bsel = nc.dram_tensor("dbg_bsel", [32, D], BF16, kind="ExternalOutput").ap()
        dbg_ar = nc.dram_tensor("dbg_ar", [128, NF * 36], BF16, kind="ExternalOutput").ap()
        dbg_rwb = nc.dram_tensor("dbg_rwb", [32, D], BF16, kind="ExternalOutput").ap()
        dbg_stb = nc.dram_tensor("dbg_stb", [32, 4], BF16, kind="ExternalOutput").ap()

    es = contextlib.ExitStack()
    es.__enter__()
    L = Ledger(es, nc)

    def sb(name, shape, dt, stack=None):
        return (stack or es).enter_context(nc.sbuf_tensor(name, shape, dt))

    ln = sb("ln", [128, NF * TH], BF16)
    bqc = sb("bqc", [128, NF], F32)
    bkc = sb("bkc", [128, NF], F32)
    bbc = sb("bbc", [128, NF], F32)
    bvb = sb("bvb", [1, D], BF16)
    bd_all = sb("bd_all", [128, NH * 128], BF16)
    ar_slab = sb("ar_slab", [128, NF * 36], BF16)
    bsel = sb("bsel", [32, D], BF16)
    zw = sb("zw", [32, T], BF16)
    logp = sb("logp", [4, 1], F32)
    rbsel = sb("rbsel", [4, 1], F32)
    ones_bf = sb("ones_bf", [128, 1], BF16)
    onesr_bf = sb("onesr_bf", [1, 128], BF16)
    onesr_f = sb("onesr_f", [1, 128], F32)
    ones16_f = sb("ones16_f", [1, 32], F32)
    ones4c_f = sb("ones4c_f", [32, 1], F32)
    zeroc = sb("zeroc", [128, 1], F32)
    p4e_c = sb("p4e_c", [32, 32], F32)
    ps = [es.enter_context(nc.psum_tensor(f"ps{i}", [128, 512], F32)) for i in range(8)]

    block = es.enter_context(nc.Block())
    BAR = L.s("bar")
    bar_n = [0]

    def barrier(free=None):
        target = bar_n[0] + 5
        freed = L.take(free) if free else []
        for eng in (block.sync, block.scalar, block.vector, block.tensor):
            @eng
            def _(e):
                e.nop().then_inc(BAR, 1)
                e.wait_ge(BAR, target)

        @block.gpsimd
        def _(g):
            for name, h, n in freed:
                if n:
                    g.wait_ge(h, n)
            if freed:
                nc.clear_and_free_semaphores([h for _, h, _ in freed])
            g.nop().then_inc(BAR, 1)
            g.wait_ge(BAR, target)
        bar_n[0] = target

    p0s = contextlib.ExitStack()
    p0s.__enter__()
    bw3 = sb("bw3", [3, 16], F32, p0s)
    act_i = sb("act_i", [1, 4], I32, p0s)
    act_f = sb("act_f", [1, 4], F32, p0s)
    prior_s = sb("prior_s", [32, 1], F32, p0s)
    rb_s = sb("rb_s", [32, 1], F32, p0s)
    routw_b = sb("routw_b", [32, D], BF16, p0s)
    lA_b = sb("lA_b", [128, D], BF16, p0s)
    lB_b = sb("lB_b", [128, D], BF16, p0s)
    bivk_s = sb("bivk_s", [3, 64], F32, p0s)
    rev_s = sb("rev_s", [8, 1], F32, p0s)
    d0_s = sb("d0_s", [8, 128], F32, p0s)
    p8_s = sb("p8_s", [8, 128], F32, p0s)
    bdm_s = sb("bdm_s", [128, 128], F32, p0s)
    p16_s = sb("p16_s", [32, 128], F32, p0s)
    dg8_s = sb("dg8_s", [128, 32], F32, p0s)
    io16_s = sb("io16_s", [32, 1], F32, p0s)
    st_f = sb("st_f", [32, 4], F32, p0s)
    st_f2 = sb("st_f2", [32, 4], F32, p0s)
    st_b = sb("st_b", [32, 4], BF16, p0s)
    wk_sb = sb("wk_sb", [16, 64], F32, p0s)
    mall = sb("mall", [8, 128], F32, p0s)
    ms2 = sb("ms2", [8, 128], F32, p0s)
    t1_sb = sb("t1_sb", [8, 128], F32, p0s)
    t1s_sb = sb("t1s_sb", [32, 128], F32, p0s)
    sel_b = sb("sel_b", [128, 32], BF16, p0s)

    @block.sync
    def _(sync):
        loads = [(bqc[:], bq_t), (bkc[:], bk_t), (bbc[:], bb_t), (p4e_c[:], p4e),
                 (bw3[:], bwT), (act_i[:], active), (prior_s[:], prior),
                 (rb_s[:], rb_in),
                 (bivk_s[:], bivk3), (rev_s[:], rev8), (d0_s[:], d0t), (p8_s[:], p8),
                 (bdm_s[:], bdmask), (p16_s[:], p16), (dg8_s[:], diag8), (io16_s[:], iota16)]
        for dst, src in loads:
            L.inc(sync.dma_start(out=dst, in_=src), "p0ld", 16)

    @block.gpsimd
    def _(g):
        L.inc(g.dma_start(out=bvb[:], in_=bv_r), "p0ldg", 16)
        L.inc(g.dma_start(out=routw_b[:], in_=routW), "p0ldg", 16)
        L.inc(g.dma_start(out=lA_b[:], in_=loraA), "p0ldg", 16)
        L.inc(g.dma_start(out=lB_b[:], in_=loraB), "p0ldg", 16)

    @block.vector
    def _(v):
        v.wait_ge(L.s("p0ld"), L.n("p0ld"))
        for ms in (v.memset(ones_bf[:], 1.0), v.memset(onesr_bf[:], 1.0),
                   v.memset(onesr_f[:], 1.0), v.memset(ones16_f[:], 1.0),
                   v.memset(ones4c_f[:], 1.0), v.memset(t1s_sb[:], 0.0),
                   v.memset(zeroc[:], 0.0)):
            L.inc(ms, "p0ms", 1)
        L.inc(v.tensor_copy(act_f[:], act_i[:]), "p0cast", 1)

    nc.const_aps.register(F32, 0.0, zeroc[:, 0:1])

    @block.tensor
    def _(t):
        t.wait_ge(L.s("p0ms"), 7)
        t.wait_ge(L.s("p0cast"), 1)
        t.matmul(ps[0][0:32, 0:4], ones16_f[0:1, :], act_f[0:1, :], start=True, stop=True)
        L.inc(t.matmul(ps[1][0:16, 0:64], bw3[:, :], bivk_s[:, :], start=True, stop=True),
              "p0mm", 1)

    @block.vector
    def _(v):
        v.wait_ge(L.s("p0mm"), 1)
        L.inc(v.tensor_scalar(st_f[:], ps[0][0:32, 0:4], io16_s[:, 0:1], None,
                              ALU.is_equal), "p0ieq", 1)
        L.inc(v.tensor_copy(wk_sb[:], ps[1][0:16, 0:64]), "p0ev", 1)

    @block.scalar
    def _(sc):
        sc.wait_ge(L.s("p0ieq"), 1)
        L.inc(sc.copy(st_b[:], st_f[:]), "p0stb", 1)
        L.inc(sc.copy(st_f2[:], st_f[:]), "p0stf", 1)

    @block.sync
    def _(sync):
        sync.wait_ge(L.s("p0ev"), 1)
        L.inc(sync.dma_start(out=wk64, in_=wk_sb[:]), "p0wka", 16)
        sync.wait_ge(L.s("p0wka"), 16)
        L.inc(sync.dma_start(out=mall[:, :].rearrange("i (h j) -> i h j", h=16),
                             in_=wk64.rearrange("h (i j) -> i h j", i=8)),
              "p0wkb", 16)

    @block.vector
    def _(v):
        v.wait_ge(L.s("p0wkb"), 16)
        v.tensor_scalar(ms2[:], mall[:], rev_s[:, 0:1], None, ALU.mult)
        L.inc(v.tensor_tensor(ms2[:], ms2[:], d0_s[:], ALU.add), "p0m2", 1)

    for h in range(NH):
        @block.tensor
        def _(t, h=h):
            if h == 0:
                t.wait_ge(L.s("p0m2"), 1)
            else:
                t.wait_ge(L.s("p0t1ev"), h)
            L.inc(t.matmul(ps[2][0:8, 0:128], ms2[:, h * 8:(h + 1) * 8], p8_s[:, :],
                           start=True, stop=True), "p0t1", 1)
            t.wait_ge(L.s("p0t1ev"), h + 1)
            if h >= 1:
                t.wait_ge(L.s("p0bd_ev"), h)
            L.inc(t.matmul(ps[3][0:128, 0:128], t1_sb[:, :], p8_s[:, :],
                           start=True, stop=True), "p0t2", 1)

        @block.vector
        def _(v, h=h):
            v.wait_ge(L.s("p0t1"), h + 1)
            L.inc(v.tensor_copy(t1_sb[:], ps[2][0:8, 0:128]), "p0t1ev", 1)
            v.wait_ge(L.s("p0t2"), h + 1)
            L.inc(v.tensor_tensor(bd_all[:, h * 128:(h + 1) * 128], ps[3][0:128, 0:128],
                                  bdm_s[:], ALU.mult), "p0bd_ev", 1)

    @block.tensor
    def _(t):
        t.wait_ge(L.s("p0stf"), 1)
        L.inc(t.matmul(ps[4][0:4, 0:128], st_f2[:, :], p16_s[:, :], start=True, stop=True),
              "p0s1", 1)
        t.wait_ge(L.s("p0s1ev"), 1)
        L.inc(t.matmul(ps[5][0:128, 0:32], t1s_sb[:, :], p4e_c[:, :],
                       start=True, stop=True), "p0s2", 1)
        t.matmul(ps[6][0:4, 0:1], st_f2[:, :], prior_s[:, :], start=True, stop=True)
        L.inc(t.matmul(ps[6][0:4, 1:2], st_f2[:, :], rb_s[:, :], start=True, stop=True),
              "p0pr", 1)

    @block.vector
    def _(v):
        v.wait_ge(L.s("p0s1"), 1)
        L.inc(v.tensor_copy(t1s_sb[0:4, :], ps[4][0:4, 0:128]), "p0s1ev", 1)
        v.wait_ge(L.s("p0s2"), 1)
        L.inc(v.tensor_tensor(sel_b[:], ps[5][0:128, 0:32], dg8_s[:], ALU.mult), "p0sel", 1)
        v.wait_ge(L.s("p0pr"), 1)
        L.inc(v.tensor_copy(rbsel[:], ps[6][0:4, 1:2]), "p0rb", 1)

    @block.scalar
    def _(sc):
        sc.wait_ge(L.s("p0pr"), 1)
        L.inc(sc.activation(logp[:], ps[6][0:4, 0:1], AF.Ln, bias=f32(1e-12)), "p0lp", 1)

    @block.tensor
    def _(t):
        t.wait_ge(L.s("p0sel"), 1)
        t.wait_ge(L.s("p0ldg"), 64)
        t.wait_ge(L.s("p0stb"), 1)
        for ob in range(8):
            if ob >= 1:
                t.wait_ge(L.s("p0bs_ev"), ob)
            L.inc(t.matmul(ps[7][0:32, :], sel_b[:, :], lB_b[:, ob * 512:(ob + 1) * 512],
                           start=True, stop=True), "p0bs", 1)
        t.wait_ge(L.s("p0bd_ev"), NH)
        t.wait_ge(L.s("p0ev"), 1)
        for ft in range(NF):
            b0 = (ft % 2) * 2
            if ft >= 2:
                t.wait_ge(L.s("p0ar_ev"), 2 * (ft - 1))
            L.inc(t.matmul(ps[b0][0:128, 0:32], lA_b[:, ft * 128:(ft + 1) * 128],
                           sel_b[:, :], start=True, stop=True), "p0ar", 1)
            L.inc(t.matmul(ps[b0 + 1][0:128, 0:4], routw_b[:, ft * 128:(ft + 1) * 128],
                           st_b[:, :], start=True, stop=True), "p0ar", 1)

    @block.vector
    def _(v):
        for ob in range(8):
            v.wait_ge(L.s("p0bs"), ob + 1)
            L.inc(v.tensor_scalar(bsel[:, ob * 512:(ob + 1) * 512], ps[7][0:32, :],
                                  2.0, None, ALU.mult), "p0bs_ev", 1)
        for ft in range(NF):
            b0 = (ft % 2) * 2
            v.wait_ge(L.s("p0ar"), 2 * ft + 2)
            L.inc(v.tensor_copy(ar_slab[:, ft * 36:ft * 36 + 32], ps[b0][0:128, 0:32]),
                  "p0ar_ev", 1)
            L.inc(v.tensor_copy(ar_slab[:, ft * 36 + 32:ft * 36 + 36],
                                ps[b0 + 1][0:128, 0:4]), "p0ar_ev", 1)

    @block.gpsimd
    def _(g):
        for s in ("p0bd_ev", "p0ar_ev", "p0bs_ev", "p0lp", "p0rb", "p0sel"):
            g.wait_ge(L.s(s), L.n(s))
        if DEBUG_SPILLS:
            L.inc(g.dma_start(out=dbg_rwb, in_=routw_b[:]), "p0dbg", 16)
            L.inc(g.dma_start(out=dbg_stb, in_=st_b[:]), "p0dbg", 16)
            g.wait_ge(L.s("p0dbg"), L.n("p0dbg"))
    barrier(free=["p0"])
    p0s.__exit__(None, None, None)

    p12 = contextlib.ExitStack()
    p12.__enter__()
    xbufA = [sb(f"xbufA{i}", [128, 512], F32, p12) for i in range(4)]
    xbufB = [sb(f"xbufB{i}", [128, 512], F32, p12) for i in range(5)]
    xb_bf = [sb(f"xb_bf{i}", [128, 512], BF16, p12) for i in range(4)]
    sq_bf = [sb(f"sq_bf{i}", [128, 512], BF16, p12) for i in range(4)]
    mrow = sb("mrow", [1, 512], F32, p12)
    rrow = sb("rrow", [1, 512], F32, p12)
    vrow_ = sb("vrow_", [1, 512], F32, p12)
    mbc = sb("mbc", [128, 512], F32, p12)
    rbc = sb("rbc", [128, 512], F32, p12)
    slab = [sb(f"slab{i}", [128, NF * 512], BF16, p12) for i in range(2)]
    evq = [sb(f"evq{i}", [128, 1024], BF16, p12) for i in range(4)]
    evv = [sb(f"evv{i}", [128, 2048], BF16, p12) for i in range(2)]
    P1KD, P1KP, P1SLOT = [0], [0], {}

    for H in range(2):
        for tt in range(2):
            t4 = H * 2 + tt
            gt = t4

            @block.sync
            def _(sync, t4=t4, gt=gt):
                for cc in range(NF):
                    k = gt * NF + cc
                    i = k % 4
                    if k >= 4:
                        sync.wait_ge(L.s("p1cp"), k - 3)
                    L.inc(sync.dma_start(
                        out=xbufA[i][:],
                        in_=xT[cc * 128:(cc + 1) * 128, t4 * 512:(t4 + 1) * 512]),
                        f"p1lA{i}", 16)

            @block.scalar
            def _(sc, gt=gt):
                for cc in range(NF):
                    k = gt * NF + cc
                    i = k % 4
                    sc.wait_ge(L.s(f"p1lA{i}"), 16 * (k // 4 + 1))
                    if k >= 4:
                        sc.wait_ge(L.s("p1mm"), 2 * (k - 3))
                    L.inc(sc.copy(xb_bf[k % 4][:], xbufA[i][:]), "p1cp", 1)

            @block.vector
            def _(v, gt=gt):
                for cc in range(NF):
                    i = cc % 4
                    k = gt * NF + cc
                    v.wait_ge(L.s("p1cp"), k + 1)
                    if k >= 4:
                        v.wait_ge(L.s("p1mm"), 2 * (k - 3))
                    L.inc(v.tensor_tensor(sq_bf[k % 4][:], xb_bf[k % 4][:],
                                          xb_bf[k % 4][:], ALU.mult), "p1sqv", 1)

            @block.tensor
            def _(t, gt=gt):
                for cc in range(NF):
                    k = gt * NF + cc
                    t.wait_ge(L.s("p1sqv"), k + 1)
                    t.matmul(ps[0][0:1, :], ones_bf[:, 0:1], xb_bf[k % 4][:],
                             start=(cc == 0), stop=(cc == NF - 1))
                    L.inc(t.matmul(ps[1][0:1, :], ones_bf[:, 0:1], sq_bf[k % 4][:],
                                   start=(cc == 0), stop=(cc == NF - 1)), "p1mm", 2)
                t.wait_ge(L.s("p1row"), gt + 1)
                t.wait_ge(L.s("p1mr"), gt + 1)
                if gt >= 1:
                    t.wait_ge(L.s("p1bcev"), gt)
                t.matmul(ps[2][:, :], onesr_f[0:1, :], mrow[0:1, :], start=True, stop=True)
                L.inc(t.matmul(ps[3][:, :], onesr_f[0:1, :], rrow[0:1, :],
                               start=True, stop=True), "p1bc", 1)

            @block.vector
            def _(v, gt=gt):
                v.wait_ge(L.s("p1mm"), 2 * NF * (gt + 1))
                L.inc(v.tensor_scalar(mrow[0:1, :], ps[0][0:1, :], 1.0 / D, None, ALU.mult),
                      "p1mr", 1)
                v.tensor_scalar(vrow_[0:1, :], ps[1][0:1, :], 1.0 / D, None, ALU.mult)
                v.tensor_tensor(rrow[0:1, :], mrow[0:1, :], mrow[0:1, :], ALU.mult)
                L.inc(v.tensor_tensor(vrow_[0:1, :], vrow_[0:1, :], rrow[0:1, :],
                                      ALU.subtract), "p1var", 1)

            @block.scalar
            def _(sc, gt=gt):
                sc.wait_ge(L.s("p1var"), gt + 1)
                sc.activation(rrow[0:1, :], vrow_[0:1, :], AF.Ln, bias=f32(EPS))
                L.inc(sc.activation(rrow[0:1, :], rrow[0:1, :], AF.Exp, scale=f32(-0.5)),
                      "p1row", 1)

            @block.scalar
            def _(sc, t4=t4, gt=gt):
                for cc in range(NF):
                    i = P1KD[0] % 5
                    u = P1KD[0] // 5
                    P1KD[0] += 1
                    P1SLOT[(gt, cc)] = (i, u)
                    if u >= 1:
                        sc.wait_ge(L.s(f"p1cB{i}"), u)
                    L.inc(sc.dma_start(
                        out=xbufB[i][:],
                        in_=xT[cc * 128:(cc + 1) * 128, t4 * 512:(t4 + 1) * 512]),
                        f"p1lB{i}", 16)

            @block.vector
            def _(v, tt=tt, gt=gt):
                v.wait_ge(L.s("p1bc"), gt + 1)
                v.tensor_copy(mbc[:], ps[2][:, :])
                L.inc(v.tensor_copy(rbc[:], ps[3][:, :]), "p1bcev", 1)
                for cc in range(NF):
                    i, u = P1SLOT[(gt, cc)]
                    v.wait_ge(L.s(f"p1lB{i}"), 16 * (u + 1))
                    v.tensor_tensor(xbufB[i][:], xbufB[i][:], mbc[:], ALU.subtract)
                    L.inc(v.tensor_tensor(
                        ln[:, cc * TH + tt * 512: cc * TH + (tt + 1) * 512],
                        xbufB[i][:], rbc[:], ALU.mult), f"p1cB{i}", 1)

        @block.gpsimd
        def _(g):
            for i in range(6):
                if f"p1cB{i}" in L.d:
                    g.wait_ge(L.s(f"p1cB{i}"), L.n(f"p1cB{i}"))
        barrier()

        weights = [("q", wq, qsp, bqc), ("k", wk, ksp, bkc), ("v", wv, vsp, None)]
        if H == 0:
            SLAB_MM = {}
        for wi, (wname, wsrc, wspill, bias) in enumerate(weights):
            for b in range(8):
                sidx = wi * 8 + b
                gs = H * 24 + sidx
                par = gs % 2

                @block.gpsimd
                def _(g, b=b, gs=gs, par=par, wsrc=wsrc):
                    if gs >= 2 and (gs - 2) in SLAB_MM:
                        g.wait_ge(L.s("p2mm"), SLAB_MM[gs - 2])
                    L.inc(g.dma_start(out=slab[par][:], in_=wsrc[b]), f"p2wl{par}", 16)

                if wname == "q":
                    def emit_q_sub(sw, b=b, gs=gs, par=par):
                        sub = sw
                        cc = b * 4 + sub

                        @block.tensor
                        def _(t, sw=sw, sub=sub, par=par, gs=gs):
                            t.wait_ge(L.s("p2evq"), max(0, L.n("p2evq") - 2))
                            t.wait_ge(L.s("p2evv"), L.n("p2evv"))
                            if sw == 0:
                                t.wait_ge(L.s(f"p2wl{par}"), 16 * (gs // 2 + 1))
                                t.wait_ge(L.s("p2bde"), L.n("p2bde"))
                            for fi in range(NF):
                                lhsT = slab[par][:, fi * 512 + sub * 128: fi * 512 + sub * 128 + 128]
                                for tt in range(2):
                                    ins = t.matmul(
                                        ps[(sw % 2) * 2 + tt][:, :], lhsT,
                                        ln[:, fi * TH + tt * 512: fi * TH + (tt + 1) * 512],
                                        start=(fi == 0), stop=(fi == NF - 1))
                            L.inc(ins, "p2mm", 1)

                        @block.vector
                        def _(v, sw=sw, sub=sub, cc=cc):
                            v.wait_ge(L.s("p2mm"), L.n("p2mm"))
                            nsp = L.n(f"p2sq{sub}")
                            if nsp:
                                v.wait_ge(L.s(f"p2sq{sub}"), nsp)
                            for tt in range(2):
                                L.inc(v.tensor_scalar(evq[sub][:, tt * 512:(tt + 1) * 512],
                                                      ps[(sw % 2) * 2 + tt][:, :],
                                                      bqc[:, cc:cc + 1], None, ALU.add),
                                      "p2evq", 1)

                    def emit_q_bd(sw, b=b, H=H):
                        sub = sw
                        cc = b * 4 + sub
                        hbd = cc // 2

                        @block.tensor
                        def _(t, sub=sub, hbd=hbd):
                            t.wait_ge(L.s("p2evq"), L.n("p2evq"))
                            t.wait_ge(L.s("p2bde"), L.n("p2bde"))
                            for half in range(2):
                                L.inc(t.matmul(ps[4 + half][:, :],
                                               bd_all[:, hbd * 128:(hbd + 1) * 128],
                                               evq[sub][:, half * 512:(half + 1) * 512],
                                               start=True, stop=True), "p2bd", 1)

                        @block.vector
                        def _(v, sub=sub):
                            v.wait_ge(L.s("p2bd"), L.n("p2bd"))
                            for half in range(2):
                                L.inc(v.tensor_copy(evq[sub][:, half * 512:(half + 1) * 512],
                                                    ps[4 + half][:, :]), "p2bde", 1)

                        @block.scalar
                        def _(sc, sub=sub, cc=cc, H=H):
                            sc.wait_ge(L.s("p2bde"), L.n("p2bde"))
                            L.inc(sc.dma_start(out=qsp[cc, :, H * TH:(H + 1) * TH],
                                               in_=evq[sub][:]), f"p2sq{sub}", 16)

                    for sw in range(4):
                        if sw >= 2:
                            emit_q_bd(sw - 2)
                        emit_q_sub(sw)
                    emit_q_bd(2)
                    emit_q_bd(3)
                    SLAB_MM[gs] = L.n("p2mm")
                    continue

                @block.tensor
                def _(t, wname=wname, b=b, gs=gs, par=par):
                    t.wait_ge(L.s(f"p2wl{par}"), 16 * (gs // 2 + 1))
                    if wname == "k":
                        for hw in range(2):
                            if gs >= 1:
                                t.wait_ge(L.s("p2evq"),
                                          max(0, L.n("p2evq") - (4 if hw == 0 else 0)))
                                t.wait_ge(L.s("p2evv"), L.n("p2evv"))
                                if hw == 1:
                                    t.wait_ge(L.s("p2bde"), L.n("p2bde"))
                            for fi in range(NF):
                                for sub in (2 * hw, 2 * hw + 1):
                                    lhsT = slab[par][:, fi * 512 + sub * 128: fi * 512 + sub * 128 + 128]
                                    for tt in range(2):
                                        ins = t.matmul(
                                            ps[sub * 2 + tt][:, :], lhsT,
                                            ln[:, fi * TH + tt * 512: fi * TH + (tt + 1) * 512],
                                            start=(fi == 0), stop=(fi == NF - 1))
                            L.inc(ins, "p2mm", 1)
                    else:
                        for hw in range(2):
                            if gs >= 1:
                                t.wait_ge(L.s("p2evv"),
                                          max(0, L.n("p2evv") - (8 if hw == 0 else 0)))
                                t.wait_ge(L.s("p2evq"), L.n("p2evq"))
                                if hw == 1:
                                    t.wait_ge(L.s("p2bde"), L.n("p2bde"))
                            for fi in range(NF):
                                for ti in range(4 * hw, 4 * hw + 4):
                                    t.matmul(ps[ti][:, :],
                                             ln[:, fi * TH + ti * 128: fi * TH + ti * 128 + 128],
                                             slab[par][:, fi * 512:(fi + 1) * 512],
                                             start=(fi == 0), stop=False)
                            for ti in range(4 * hw, 4 * hw + 4):
                                ins = t.matmul(ps[ti][:, :], onesr_bf[0:1, 0:128],
                                               bvb[0:1, b * 512:(b + 1) * 512],
                                               start=False, stop=True)
                            L.inc(ins, "p2mm", 1)

                evq_base = L.n("p2evq")
                evv_base = L.n("p2evv")

                @block.vector
                def _(v, wname=wname, b=b, gs=gs):
                    if wname == "k":
                        for hw in range(2):
                            v.wait_ge(L.s("p2mm"), L.n("p2mm") - (1 if hw == 0 else 0))
                            for sub in (2 * hw, 2 * hw + 1):
                                cc = b * 4 + sub
                                nsp = L.n(f"p2sq{sub}")
                                if nsp:
                                    v.wait_ge(L.s(f"p2sq{sub}"), nsp)
                                for tt in range(2):
                                    L.inc(v.tensor_scalar(evq[sub][:, tt * 512:(tt + 1) * 512],
                                                          ps[sub * 2 + tt][:, :],
                                                          bias[:, cc:cc + 1], None, ALU.add),
                                          "p2evq", 1)
                    else:
                        sl0, sl1 = 0, 1
                        for hw in range(2):
                            v.wait_ge(L.s("p2mm"), L.n("p2mm") - (1 if hw == 0 else 0))
                            if hw == 0:
                                for sl in (sl0, sl1):
                                    nsp = L.n(f"p2sv{sl}")
                                    if nsp:
                                        v.wait_ge(L.s(f"p2sv{sl}"), nsp)
                            for ti in range(4 * hw, 4 * hw + 4):
                                L.inc(v.tensor_copy(evv[sl0][:, ti * 256:(ti + 1) * 256],
                                                    ps[ti][:, 0:256]), "p2evv", 1)
                                L.inc(v.tensor_copy(evv[sl1][:, ti * 256:(ti + 1) * 256],
                                                    ps[ti][:, 256:512]), "p2evv", 1)

                @block.scalar
                def _(sc, wname=wname, b=b, H=H, wspill=wspill,
                      evq_base=evq_base, evv_base=evv_base):
                    if wname == "k":
                        for sub in range(4):
                            cc = b * 4 + sub
                            sc.wait_ge(L.s("p2evq"), evq_base + 2 * (sub + 1))
                            L.inc(sc.dma_start(out=wspill[cc, :, H * TH:(H + 1) * TH],
                                               in_=evq[sub][:]), f"p2sq{sub}", 16)
                    else:
                        h0, h1 = 2 * b, 2 * b + 1
                        sl0, sl1 = 0, 1
                        sc.wait_ge(L.s("p2evv"), evv_base + 16)
                        L.inc(sc.dma_start(out=vsp[h0, :, H * 2048:(H + 1) * 2048],
                                           in_=evv[sl0][:]), f"p2sv{sl0}", 16)
                        L.inc(sc.dma_start(out=vsp[h1, :, H * 2048:(H + 1) * 2048],
                                           in_=evv[sl1][:]), f"p2sv{sl1}", 16)

                SLAB_MM[gs] = L.n("p2mm")

        @block.gpsimd
        def _(g):
            for sname in ("p2sq0", "p2sq1", "p2sq2", "p2sq3", "p2sv0", "p2sv1", "p2sv2", "p2sv3"):
                if sname in L.d:
                    g.wait_ge(L.s(sname), L.n(sname))
        barrier(free=(["p1", "p2"] if H == 1 else None))

    p12.__exit__(None, None, None)

    rowsA = sb("rowsA", [128, T], F32)
    rowsB = sb("rowsB", [128, T], F32)
    rows_m = rowsA[0:1, :]
    rows_r = rowsB[0:1, :]
    st2_s = rowsA[32:33, :]
    st2_q = rowsA[64:65, :]
    st3_s = rowsB[32:33, :]
    st3_q = rowsB[64:65, :]

    p3s = contextlib.ExitStack()
    p3s.__enter__()
    qt = [[sb(f"qt{p}_{c}", [128, T], BF16, p3s) for c in range(2)] for p in range(2)]
    kt = [[sb(f"kt{p}_{c}", [128, T], BF16, p3s) for c in range(2)] for p in range(2)]
    vt = [sb(f"vt{p}", [128, 4096], BF16, p3s) for p in range(2)]
    atile = [sb(f"atile{i}", [128, 512], BF16, p3s) for i in range(6)]
    n_sb = sb("n_sb", [1, 512], F32, p3s)
    nb_sb = sb("nb_sb", [128, 512], F32, p3s)
    otile = [sb(f"otile{i}", [128, 512], F32, p3s) for i in range(4)]
    obf = [sb(f"obf{i}", [128, 512], BF16, p3s) for i in range(2)]
    sqf = [sb(f"sqf{i}", [128, 512], BF16, p3s) for i in range(2)]

    @block.vector
    def _(v):
        L.inc(v.memset(st2_s[0:1, :], 0.0), "p3z0", 1)
        L.inc(v.memset(st2_q[0:1, :], 0.0), "p3z0", 1)

    for h in range(NH):
        p = h % 2

        @block.sync
        def _(sync, h=h, p=p):
            if h >= 2:
                sync.wait_ge(L.s("p3s"), 64 * (h - 1))
                sync.wait_ge(L.s("p3av"), 64 * (h - 1))
            for c in range(2):
                L.inc(sync.dma_start(out=qt[p][c][:], in_=qsp[2 * h + c]), f"p3ld{p}", 16)
                L.inc(sync.dma_start(out=kt[p][c][:], in_=ksp[2 * h + c]), f"p3ld{p}", 16)
            L.inc(sync.dma_start(out=vt[p][:], in_=vsp[h]), f"p3ld{p}", 16)

        @block.tensor
        def _(t, h=h, p=p):
            t.wait_ge(L.s(f"p3ld{p}"), 16 * 5 * (h // 2 + 1))
            SB = (0, 1, 5)

            def emit_S(t, tq, tc):
                gq = h * 4 + tq
                gidx = gq * 16 + tc
                if gidx >= 3:
                    t.wait_ge(L.s("p3e"), gidx - 2)
                for c in range(2):
                    sins = t.matmul(ps[SB[gidx % 3]][:, :],
                                    kt[p][c][:, tc * 128:(tc + 1) * 128],
                                    qt[p][c][:, tq * 512:(tq + 1) * 512],
                                    start=(c == 0), stop=(c == 1))
                L.inc(sins, "p3s", 1)

            def emit_nb(t, gq):
                t.wait_ge(L.s("p3rec"), gq + 1)
                t.wait_ge(L.s("p3sta"), gq)
                L.inc(t.matmul(ps[6][:, :], onesr_f[0:1, :], n_sb[0:1, :],
                               start=True, stop=True), "p3nb", 1)

            def emit_stats(t, gq):
                t.wait_ge(L.s("p3nbe"), gq + 1)
                for sub in range(2):
                    t.wait_ge(L.s("p3sq2"), 2 * gq + sub + 1)
                    t.matmul(ps[6][0:1, :], ones_bf[:, 0:1], obf[sub][:],
                             start=(sub == 0), stop=(sub == 1))
                    L.inc(t.matmul(ps[7][0:1, :], ones_bf[:, 0:1], sqf[sub][:],
                                   start=(sub == 0), stop=(sub == 1)), "p3stm", 1)

            for tq in range(4):
                gq = h * 4 + tq
                emit_S(t, tq, 0)
                emit_S(t, tq, 1)
                if tq >= 1:
                    emit_nb(t, gq - 1)
                emit_S(t, tq, 2)
                emit_S(t, tq, 3)
                for step in range(4, 20):
                    if step <= 15:
                        emit_S(t, tq, step)
                    if step == 8 and tq >= 1:
                        emit_stats(t, gq - 1)
                    tc = step - 4
                    gidx = gq * 16 + tc
                    if tc % 2 == 0:
                        t.wait_ge(L.s("p3e"), gidx + 2)
                    if tc == 0:
                        t.wait_ge(L.s("p3rec"), gq)
                        t.wait_ge(L.s("p3oev"), 2 * gq)
                    rs = t.matmul(ps[2][0:1, :], ones_bf[:, 0:1], atile[gidx % 6][:],
                                  start=(tc == 0), stop=(tc == 15))
                    if tc == 15:
                        L.inc(rs, "p3ns", 1)
                    for sub in range(2):
                        ains = t.matmul(ps[3 + sub][:, :],
                                        vt[p][:, tc * 256 + sub * 128: tc * 256 + sub * 128 + 128],
                                        atile[gidx % 6][:],
                                        start=(tc == 0), stop=(tc == 15))
                    L.inc(ains, "p3av", 1)
            emit_nb(t, h * 4 + 3)
            emit_stats(t, h * 4 + 3)

        @block.scalar
        def _(sc, h=h):
            for tq in range(4):
                gq = h * 4 + tq
                for tc in range(16):
                    gidx = gq * 16 + tc
                    sc.wait_ge(L.s("p3s"), gidx + 1)
                    if gidx >= 6:
                        sc.wait_ge(L.s("p3av"), gidx - 5)
                    L.inc(sc.activation(atile[gidx % 6][:], ps[(0, 1, 5)[gidx % 3]][:, :],
                                        AF.Exp, bias=f32(EXP_SHIFT)), "p3e", 1)
                sc.wait_ge(L.s("p3ns"), gq + 1)
                if gq >= 1:
                    sc.wait_ge(L.s("p3nb"), gq)
                sc.activation(n_sb[0:1, :], ps[2][0:1, :], AF.Ln)
                L.inc(sc.activation(n_sb[0:1, :], n_sb[0:1, :], AF.Exp, scale=f32(-1.0)),
                      "p3rec", 1)

        @block.vector
        def _(v, h=h):
            for tq in range(4):
                gq = h * 4 + tq
                v.wait_ge(L.s("p3rec"), gq + 1)
                v.wait_ge(L.s("p3nb"), gq + 1)
                L.inc(v.tensor_copy(nb_sb[:], ps[6][:, :]), "p3nbe", 1)
                for sub in range(2):
                    osl = (2 * gq + sub) % 4
                    uses = (2 * gq + sub) // 4
                    if uses:
                        v.wait_ge(L.s(f"p3sp{osl}"), 16 * uses)
                    L.inc(v.tensor_tensor(otile[osl][:], ps[3 + sub][:, :], nb_sb[:],
                                          ALU.mult), "p3oev", 1)
                    if 2 * gq + sub >= 2:
                        v.wait_ge(L.s("p3stm"), 2 * gq + sub - 1)
                    L.inc(v.tensor_copy(obf[sub][:], otile[osl][:]), "p3cp", 1)
                    L.inc(v.tensor_tensor(sqf[sub][:], obf[sub][:], obf[sub][:],
                                          ALU.mult), "p3sq2", 1)
                v.wait_ge(L.s("p3stm"), 2 * gq + 2)
                if gq == 0:
                    v.wait_ge(L.s("p3z0"), 2)
                v.tensor_tensor(st2_s[0:1, tq * 512:(tq + 1) * 512],
                                st2_s[0:1, tq * 512:(tq + 1) * 512], ps[6][0:1, :], ALU.add)
                L.inc(v.tensor_tensor(st2_q[0:1, tq * 512:(tq + 1) * 512],
                                      st2_q[0:1, tq * 512:(tq + 1) * 512], ps[7][0:1, :],
                                      ALU.add), "p3sta", 1)

        @block.gpsimd
        def _(g, h=h):
            for tq in range(4):
                gq = h * 4 + tq
                for sub in range(2):
                    osl = (2 * gq + sub) % 4
                    cc = 2 * h + sub
                    g.wait_ge(L.s("p3oev"), 2 * gq + sub + 1)
                    g.wait_ge(L.s("p3cp"), 2 * gq + sub + 1)
                    L.inc(g.dma_start(out=aosp[cc, :, tq * 512:(tq + 1) * 512],
                                      in_=otile[osl][:]), f"p3sp{osl}", 16)

    @block.gpsimd
    def _(g):
        for i in range(4):
            if f"p3sp{i}" in L.d:
                g.wait_ge(L.s(f"p3sp{i}"), L.n(f"p3sp{i}"))
        g.wait_ge(L.s("p3sta"), L.n("p3sta"))
    barrier(free=["p3"])
    p3s.__exit__(None, None, None)

    p4s = contextlib.ExitStack()
    p4s.__enter__()
    aob = [sb(f"aob{i}", [128, 512], F32, p4s) for i in range(6)]
    xrb = [sb(f"xrb{i}", [128, 512], F32, p4s) for i in range(6)]
    x2b = [sb(f"x2b{i}", [128, 512], F32, p4s) for i in range(6)]
    P4KD, P4KP, P4SLOT, P4PREV = [0], [0], {}, {}
    mb2 = sb("mb2", [128, 512], F32, p4s)
    rb2 = sb("rb2", [128, 512], F32, p4s)
    x2bf = [sb(f"x2bf{i}", [128, 512], BF16, p4s) for i in range(4)]
    rtmp_t = sb("rtmp_t", [1, T], F32, p4s)
    rtmp = rtmp_t[0:1, :]
    sq2bf = [sb(f"sq2bf{i}", [128, 512], BF16, p4s) for i in range(4)]

    @block.vector
    def _(v):
        L.inc(v.tensor_scalar(rows_m[0:1, :], st2_s[0:1, :], 1.0 / D, None, ALU.mult),
              "r2m", 1)
        v.tensor_scalar(rtmp[0:1, :], st2_q[0:1, :], 1.0 / D, None, ALU.mult)
        v.tensor_tensor(rows_r[0:1, :], rows_m[0:1, :], rows_m[0:1, :], ALU.mult)
        L.inc(v.tensor_tensor(rows_r[0:1, :], rtmp[0:1, :], rows_r[0:1, :], ALU.subtract),
              "p4v2", 1)

    @block.scalar
    def _(sc):
        sc.wait_ge(L.s("p4v2"), 1)
        sc.activation(rows_r[0:1, :], rows_r[0:1, :], AF.Ln, bias=f32(EPS))
        L.inc(sc.activation(rows_r[0:1, :], rows_r[0:1, :], AF.Exp, scale=f32(-0.5)),
              "p4r2", 1)

    _p4last = {}
    for _t4 in range(4):
        for _cc in range(NF):
            _i = P4KD[0] % 6
            _u = P4KD[0] // 6
            P4KD[0] += 1
            P4SLOT[(_t4, _cc)] = (_i, _u)
            P4PREV[(_t4, _cc)] = _p4last.get(_i, -1)
            _p4last[_i] = _t4 * NF + _cc

    def _p4issue_xrb(sc, kk):
        t4b, ccb = divmod(kk, NF)
        i, u = P4SLOT[(t4b, ccb)]
        if u >= 1:
            sc.wait_ge(L.s(f"p4x2{i}"), u)
        L.inc(sc.dma_start(out=xrb[i][:],
                           in_=xT[ccb * 128:(ccb + 1) * 128, t4b * 512:(t4b + 1) * 512]),
              f"p4lB{i}", 16)

    for t4 in range(4):
        @block.tensor
        def _(t, t4=t4):
            t.wait_ge(L.s("p4r2"), 1)
            t.wait_ge(L.s("r2m"), 1)
            if t4 >= 1:
                t.wait_ge(L.s("p4bcev"), t4)
            t.matmul(ps[0][:, :], onesr_f[0:1, :], rows_m[0:1, t4 * 512:(t4 + 1) * 512],
                     start=True, stop=True)
            L.inc(t.matmul(ps[1][:, :], onesr_f[0:1, :], rows_r[0:1, t4 * 512:(t4 + 1) * 512],
                           start=True, stop=True), "p4bc", 1)

        @block.sync
        def _(sync, t4=t4):
            for cc in range(NF):
                i, u = P4SLOT[(t4, cc)]
                if u >= 1:
                    sync.wait_ge(L.s(f"p4x2{i}"), u)
                L.inc(sync.dma_start(out=aob[i][:],
                                     in_=aosp[cc, :, t4 * 512:(t4 + 1) * 512]), f"p4lA{i}", 16)

        @block.vector
        def _(v, t4=t4):
            v.wait_ge(L.s("p4bc"), t4 + 1)
            v.tensor_copy(mb2[:], ps[0][:, :])
            L.inc(v.tensor_copy(rb2[:], ps[1][:, :]), "p4bcev", 1)
            def p4sq(v, j):
                v.wait_ge(L.s("p4cp"), j + 1)
                if j >= 4:
                    v.wait_ge(L.s("p4mm"), 2 * (j - 3))
                L.inc(v.tensor_tensor(sq2bf[j % 4][:], x2bf[j % 4][:], x2bf[j % 4][:],
                                      ALU.mult), "p4sqv", 1)

            for cc in range(NF):
                i, u = P4SLOT[(t4, cc)]
                k = t4 * NF + cc
                v.wait_ge(L.s(f"p4lA{i}"), 16 * (u + 1))
                v.wait_ge(L.s(f"p4lB{i}"), 16 * (u + 1))
                if u >= 1:
                    v.wait_ge(L.s(f"p4sp{i}"), 16 * u)
                    v.wait_ge(L.s("p4cp"), P4PREV[(t4, cc)] + 1)
                v.tensor_tensor(aob[i][:], aob[i][:], mb2[:], ALU.subtract)
                v.tensor_tensor(aob[i][:], aob[i][:], rb2[:], ALU.mult)
                L.inc(v.tensor_tensor(x2b[i][:], aob[i][:], xrb[i][:], ALU.add),
                      f"p4x2{i}", 1)
                if cc >= 3:
                    p4sq(v, k - 3)
            for cc in range(NF - 3, NF):
                p4sq(v, t4 * NF + cc)

        @block.gpsimd
        def _(g, t4=t4):
            for cc in range(NF):
                i, u = P4SLOT[(t4, cc)]
                g.wait_ge(L.s(f"p4x2{i}"), u + 1)
                L.inc(g.dma_start(out=x2sp[cc, :, t4 * 512:(t4 + 1) * 512],
                                  in_=x2b[i][:]), f"p4sp{i}", 16)

        @block.scalar
        def _(sc, t4=t4):
            for cc in range(NF):
                i, u = P4SLOT[(t4, cc)]
                k = t4 * NF + cc
                if k == 0:
                    for kk in range(4):
                        _p4issue_xrb(sc, kk)
                if k + 4 < 4 * NF:
                    _p4issue_xrb(sc, k + 4)
                sc.wait_ge(L.s(f"p4x2{i}"), u + 1)
                if k >= 4:
                    sc.wait_ge(L.s("p4mm"), 2 * (k - 3))
                L.inc(sc.copy(x2bf[k % 4][:], x2b[i][:]), "p4cp", 1)

        @block.tensor
        def _(t, t4=t4):
            for cc in range(NF):
                k = t4 * NF + cc
                t.wait_ge(L.s("p4sqv"), k + 1)
                t.matmul(ps[2][0:1, :], ones_bf[:, 0:1], x2bf[k % 4][:],
                         start=(cc == 0), stop=(cc == NF - 1))
                L.inc(t.matmul(ps[3][0:1, :], ones_bf[:, 0:1], sq2bf[k % 4][:],
                               start=(cc == 0), stop=(cc == NF - 1)), "p4mm", 2)

        @block.vector
        def _(v, t4=t4):
            v.wait_ge(L.s("p4mm"), 2 * NF * (t4 + 1))
            v.tensor_copy(st3_s[0:1, t4 * 512:(t4 + 1) * 512], ps[2][0:1, :])
            L.inc(v.tensor_copy(st3_q[0:1, t4 * 512:(t4 + 1) * 512], ps[3][0:1, :]),
                  "p4st3", 1)

    @block.vector
    def _(v):
        v.wait_ge(L.s("p4st3"), 4)
        L.inc(v.tensor_scalar(rows_m[0:1, :], st3_s[0:1, :], 1.0 / D, None, ALU.mult),
              "r3m", 1)
        v.tensor_scalar(rtmp[0:1, :], st3_q[0:1, :], 1.0 / D, None, ALU.mult)
        v.tensor_tensor(rows_r[0:1, :], rows_m[0:1, :], rows_m[0:1, :], ALU.mult)
        L.inc(v.tensor_tensor(rows_r[0:1, :], rtmp[0:1, :], rows_r[0:1, :], ALU.subtract),
              "p4v3", 1)

    @block.scalar
    def _(sc):
        sc.wait_ge(L.s("p4v3"), 1)
        sc.activation(rows_r[0:1, :], rows_r[0:1, :], AF.Ln, bias=f32(EPS))
        L.inc(sc.activation(rows_r[0:1, :], rows_r[0:1, :], AF.Exp, scale=f32(-0.5)),
              "p4r3", 1)

    @block.gpsimd
    def _(g):
        for i in range(6):
            if f"p4sp{i}" in L.d:
                g.wait_ge(L.s(f"p4sp{i}"), L.n(f"p4sp{i}"))
        g.wait_ge(L.s("p4r3"), 1)
    barrier(free=["p4b", "p4c", "p4l", "p4m", "p4s", "p4v", "p4x", "p4r2"])
    p4s.__exit__(None, None, None)

    p5s = contextlib.ExitStack()
    p5s.__enter__()
    x2c = [sb(f"x2c{i}", [128, 512], F32, p5s) for i in range(6)]
    mb3 = sb("mb3", [128, 512], F32, p5s)
    rb3 = sb("rb3", [128, 512], F32, p5s)
    slab5 = [sb(f"slab5{i}", [128, NF * 512], BF16, p5s) for i in range(2)]
    evf = [sb(f"evf{i}", [128, 512], F32, p5s) for i in range(4)]
    z_sb = sb("z_sb", [32, 512], F32, p5s)
    al_t = sb("al_t", [32, 1536], F32, p5s)
    al_in = al_t[0:4, 0:512]
    al_ex = al_t[0:4, 512:1024]
    al_r = al_t[0:1, 1024:1536]
    al_f = al_t[0:4, 0:512]
    al_ex32 = al_t[0:32, 512:1024]
    al_f32 = al_t[0:32, 0:512]

    @block.vector
    def _(v):
        L.inc(v.memset(al_t[:], 0.0), "p5z0", 1)

    def _d1slot(k):
        return (k % 2) * 3 + (k // 2) % 3

    for H in range(2):
        for tt in range(2):
            t4 = H * 2 + tt
            g5 = t4

            @block.tensor
            def _(t, t4=t4, g5=g5):
                t.wait_ge(L.s("p4r3"), 1)
                t.wait_ge(L.s("r3m"), 1)
                t.wait_ge(L.s("p5ev"), L.n("p5ev"))
                if g5 >= 1:
                    t.wait_ge(L.s("p5bcev"), g5)
                t.matmul(ps[0][:, :], onesr_f[0:1, :], rows_m[0:1, t4 * 512:(t4 + 1) * 512],
                         start=True, stop=True)
                L.inc(t.matmul(ps[1][:, :], onesr_f[0:1, :],
                               rows_r[0:1, t4 * 512:(t4 + 1) * 512],
                               start=True, stop=True), "p5bc", 1)

            @block.sync
            def _(sync, t4=t4, g5=g5):
                for cc in range(0, NF, 2):
                    k = g5 * NF + cc
                    i = _d1slot(k)
                    if k >= 6:
                        sync.wait_ge(L.s(f"p5cx{i}"), k // 6)
                    L.inc(sync.dma_start(out=x2c[i][:],
                                         in_=x2sp[cc, :, t4 * 512:(t4 + 1) * 512]), f"p5lx{i}", 16)

            @block.gpsimd
            def _(g, t4=t4, g5=g5):
                for cc in range(1, NF, 2):
                    k = g5 * NF + cc
                    i = _d1slot(k)
                    if k >= 6:
                        g.wait_ge(L.s(f"p5cx{i}"), k // 6)
                    L.inc(g.dma_start(out=x2c[i][:],
                                      in_=x2sp[cc, :, t4 * 512:(t4 + 1) * 512]), f"p5lx{i}", 16)

            @block.vector
            def _(v, tt=tt, g5=g5):
                v.wait_ge(L.s("p5bc"), g5 + 1)
                v.tensor_copy(mb3[:], ps[0][:, :])
                L.inc(v.tensor_copy(rb3[:], ps[1][:, :]), "p5bcev", 1)
                for cc in range(NF):
                    k = g5 * NF + cc
                    i = _d1slot(k)
                    v.wait_ge(L.s(f"p5lx{i}"), 16 * (k // 6 + 1))
                    v.tensor_tensor(x2c[i][:], x2c[i][:], mb3[:], ALU.subtract)
                    L.inc(v.tensor_tensor(ln[:, cc * TH + tt * 512: cc * TH + (tt + 1) * 512],
                                          x2c[i][:], rb3[:], ALU.mult), f"p5cx{i}", 1)


        for tt in range(2):
            gz = H * 2 + tt

            @block.tensor
            def _(t, tt=tt, gz=gz):
                t.wait_ge(L.s("p5ev"), L.n("p5ev"))
                if gz >= 1:
                    t.wait_ge(L.s("p5zev"), gz)
                for cc in range(NF):
                    k = gz * NF + cc
                    t.wait_ge(L.s(f"p5cx{_d1slot(k)}"), k // 6 + 1)
                    ins = t.matmul(ps[2][0:36, :], ar_slab[:, cc * 36:(cc + 1) * 36],
                                   ln[:, cc * TH + tt * 512: cc * TH + (tt + 1) * 512],
                                   start=(cc == 0), stop=(cc == NF - 1))
                L.inc(ins, "p5z", 1)
                t.wait_ge(L.s("p5ae"), gz + 1)
                t.wait_ge(L.s("p5z0"), 1)
                if gz >= 1:
                    t.wait_ge(L.s("p5ar", ), gz)
                L.inc(t.matmul(ps[3][0:1, :], ones4c_f[:, 0:1], al_ex32,
                               start=True, stop=True), "p5as", 1)
                t.wait_ge(L.s("p5ar"), gz + 1)
                if gz >= 1:
                    t.wait_ge(L.s("p5af"), gz)
                L.inc(t.matmul(ps[4][0:4, :], onesr_f[0:1, 0:4], al_r,
                               start=True, stop=True), "p5ab", 1)
                t.wait_ge(L.s("p5af"), gz + 1)
                if gz >= 1:
                    t.wait_ge(L.s("p5zw"), gz)
                L.inc(t.matmul(ps[5][0:32, :], p4e_c[:, :], al_f32,
                               start=True, stop=True), "p5ax", 1)

            @block.vector
            def _(v, tt=tt, gz=gz, H=H):
                v.wait_ge(L.s("p5z"), gz + 1)
                v.tensor_copy(z_sb[:], ps[2][0:32, :])
                L.inc(v.tensor_scalar(al_in, ps[2][32:36, :], rbsel[:, 0:1],
                                      logp[:, 0:1], ALU.add, ALU.add), "p5zev", 1)
                pass
                v.wait_ge(L.s("p5ab"), gz + 1)
                L.inc(v.tensor_tensor(al_f, al_ex, ps[4][0:4, :], ALU.mult), "p5af", 1)
                v.wait_ge(L.s("p5ax"), gz + 1)
                L.inc(v.tensor_tensor(zw[:, gz * 512:(gz + 1) * 512],
                                      z_sb[:], ps[5][0:32, :], ALU.mult), "p5zw", 1)

            @block.scalar
            def _(sc, gz=gz):
                sc.wait_ge(L.s("p5zev"), gz + 1)
                L.inc(sc.activation(al_ex, al_in, AF.Exp), "p5ae", 1)
                sc.wait_ge(L.s("p5as"), gz + 1)
                sc.activation(al_r, ps[3][0:1, :], AF.Ln)
                L.inc(sc.activation(al_r, al_r, AF.Exp, scale=f32(-1.0)), "p5ar", 1)


        @block.gpsimd
        def _(g):
            for i in range(6):
                if f"p5cx{i}" in L.d:
                    g.wait_ge(L.s(f"p5cx{i}"), L.n(f"p5cx{i}"))
        barrier()

        for b in range(8):
            gs = H * 8 + b
            par = gs % 2

            @block.gpsimd
            def _(g, b=b, gs=gs, par=par):
                if gs >= 2:
                    g.wait_ge(L.s("p5mm"), 2 * (gs - 1))
                L.inc(g.dma_start(out=slab5[par][:], in_=wbs[b]), f"p5wl{par}", 16)

            @block.tensor
            def _(t, b=b, gs=gs, par=par, H=H):
                t.wait_ge(L.s(f"p5wl{par}"), 16 * (gs // 2 + 1))
                for hw in range(2):
                    t.wait_ge(L.s("p5ev"),
                              max(0, L.n("p5ev") - (4 if hw == 0 else 0)))
                    for fi in range(NF):
                        for sub in (2 * hw, 2 * hw + 1):
                            lhsT = slab5[par][:, fi * 512 + sub * 128: fi * 512 + sub * 128 + 128]
                            for tt in range(2):
                                t.matmul(ps[sub * 2 + tt][:, :], lhsT,
                                         ln[:, fi * TH + tt * 512: fi * TH + (tt + 1) * 512],
                                         start=(fi == 0), stop=False)
                    t.wait_ge(L.s("p5zw"), 2 * (H + 1))
                    for sub in (2 * hw, 2 * hw + 1):
                        for tt in range(2):
                            ins = t.matmul(ps[sub * 2 + tt][:, :],
                                           bsel[:, b * 512 + sub * 128: b * 512 + sub * 128 + 128],
                                           zw[:, (H * 2 + tt) * 512:(H * 2 + tt + 1) * 512],
                                           start=False, stop=True)
                    L.inc(ins, "p5mm", 1)

            @block.vector
            def _(v, b=b, gs=gs):
                for sub in range(4):
                    for tt in range(2):
                        if tt == 0 and sub % 2 == 0:
                            v.wait_ge(L.s("p5mm"), 2 * gs + sub // 2 + 1)
                        i = sub * 2 + tt
                        cc = b * 4 + sub
                        gt_ = gs * 8 + i
                        sl = gt_ % 4
                        if gt_ >= 4:
                            v.wait_ge(L.s(f"p5sp{sl}"), 16 * (gt_ // 4))
                        L.inc(v.tensor_scalar(evf[sl][:], ps[i][:, :], bbc[:, cc:cc + 1],
                                              None, ALU.add), "p5ev", 1)

            @block.scalar
            def _(sc, b=b, gs=gs, H=H):
                base = L.n("p5ev") - 8
                for sub in range(4):
                    for tt in range(2):
                        i = sub * 2 + tt
                        cc = b * 4 + sub
                        t4 = H * 2 + tt
                        gt_ = gs * 8 + i
                        sl = gt_ % 4
                        sc.wait_ge(L.s("p5ev"), base + i + 1)
                        L.inc(sc.dma_start(out=ffsp[cc, :, t4 * 512:(t4 + 1) * 512],
                                           in_=evf[sl][:]), f"p5sp{sl}", 16)

    @block.gpsimd
    def _(g):
        for i in range(4):
            if f"p5sp{i}" in L.d:
                g.wait_ge(L.s(f"p5sp{i}"), L.n(f"p5sp{i}"))
    barrier()
    p5s.__exit__(None, None, None)

    p6s = contextlib.ExitStack()
    p6s.__enter__()
    ffb = [sb(f"ffb{i}", [128, 512], F32, p6s) for i in range(6)]
    x2r = [sb(f"x2r{i}", [128, 512], F32, p6s) for i in range(6)]
    ob6 = [sb(f"ob6{i}", [128, 512], F32, p6s) for i in range(6)]
    P6KD, P6KP, P6SLOT = [0], [0], {}
    mb4 = sb("mb4", [128, 512], F32, p6s)
    rb4 = sb("rb4", [128, 512], F32, p6s)
    fbf = [sb(f"fbf{i}", [128, 512], BF16, p6s) for i in range(6)]
    sq4 = [sb(f"sq4{i}", [128, 512], BF16, p6s) for i in range(6)]
    rtmp_t6 = sb("rtmp_t6", [1, T], F32, p6s)
    rtmp6 = rtmp_t6[0:1, :]

    for t4 in range(4):
        @block.sync
        def _(sync, t4=t4):
            for cc in range(NF):
                k = t4 * NF + cc
                if k % 2 != 0:
                    continue
                i = _d1slot(k)
                if k >= 6:
                    sync.wait_ge(L.s("p6cp"), k - 5)
                L.inc(sync.dma_start(out=ffb[i][:],
                                     in_=ffsp[cc, :, t4 * 512:(t4 + 1) * 512]), f"p6lf{i}", 16)

        @block.gpsimd
        def _(g, t4=t4):
            for cc in range(NF):
                k = t4 * NF + cc
                if k % 2 != 1:
                    continue
                i = _d1slot(k)
                if k >= 6:
                    g.wait_ge(L.s("p6cp"), k - 5)
                L.inc(g.dma_start(out=ffb[i][:],
                                  in_=ffsp[cc, :, t4 * 512:(t4 + 1) * 512]), f"p6lf{i}", 16)

        @block.scalar
        def _(sc, t4=t4):
            for cc in range(NF):
                k = t4 * NF + cc
                i = _d1slot(k)
                sc.wait_ge(L.s(f"p6lf{i}"), 16 * (k // 6 + 1))
                if k >= 6:
                    sc.wait_ge(L.s("p6mm"), 2 * (k - 5))
                L.inc(sc.copy(fbf[k % 6][:], ffb[i][:]), "p6cp", 1)

        @block.vector
        def _(v, t4=t4):
            for cc in range(NF):
                k = t4 * NF + cc
                v.wait_ge(L.s("p6cp"), k + 1)
                if k >= 6:
                    v.wait_ge(L.s("p6mm"), 2 * (k - 5))
                L.inc(v.tensor_tensor(sq4[k % 6][:], fbf[k % 6][:], fbf[k % 6][:],
                                      ALU.mult), "p6sqv", 1)

        @block.tensor
        def _(t, t4=t4):
            for cc in range(NF):
                k = t4 * NF + cc
                t.wait_ge(L.s("p6sqv"), k + 1)
                t.matmul(ps[0][0:1, :], ones_bf[:, 0:1], fbf[k % 6][:],
                         start=(cc == 0), stop=(cc == NF - 1))
                L.inc(t.matmul(ps[1][0:1, :], ones_bf[:, 0:1], sq4[k % 6][:],
                               start=(cc == 0), stop=(cc == NF - 1)), "p6mm", 2)

        @block.vector
        def _(v, t4=t4):
            v.wait_ge(L.s("p6mm"), 2 * NF * (t4 + 1))
            v.tensor_copy(st3_s[0:1, t4 * 512:(t4 + 1) * 512], ps[0][0:1, :])
            L.inc(v.tensor_copy(st3_q[0:1, t4 * 512:(t4 + 1) * 512], ps[1][0:1, :]),
                  "p6st", 1)

    @block.vector
    def _(v):
        v.wait_ge(L.s("p6st"), 4)
        L.inc(v.tensor_scalar(rows_m[0:1, :], st3_s[0:1, :], 1.0 / D, None, ALU.mult),
              "r4m", 1)
        v.tensor_scalar(rtmp6[0:1, :], st3_q[0:1, :], 1.0 / D, None, ALU.mult)
        v.tensor_tensor(rows_r[0:1, :], rows_m[0:1, :], rows_m[0:1, :], ALU.mult)
        L.inc(v.tensor_tensor(rows_r[0:1, :], rtmp6[0:1, :], rows_r[0:1, :], ALU.subtract),
              "p6v4", 1)

    @block.scalar
    def _(sc):
        sc.wait_ge(L.s("p6v4"), 1)
        sc.activation(rows_r[0:1, :], rows_r[0:1, :], AF.Ln, bias=f32(EPS))
        L.inc(sc.activation(rows_r[0:1, :], rows_r[0:1, :], AF.Exp, scale=f32(-0.5)),
              "p6r4", 1)

    for _t4 in range(4):
        for _cc in range(NF):
            _i = P6KD[0] % 6
            _u = P6KD[0] // 6
            P6KD[0] += 1
            P6SLOT[(_t4, _cc)] = (_i, _u)

    def _p6issue_x2r(g, kk):
        t4b, ccb = divmod(kk, NF)
        i, u = P6SLOT[(t4b, ccb)]
        if u >= 1:
            g.wait_ge(L.s(f"p6ou{i}"), u)
        L.inc(g.dma_start(out=x2r[i][:],
                          in_=x2sp[ccb, :, t4b * 512:(t4b + 1) * 512]), f"p6lx{i}", 16)

    for t4 in range(4):
        @block.tensor
        def _(t, t4=t4):
            t.wait_ge(L.s("p6r4"), 1)
            t.wait_ge(L.s("r4m"), 1)
            if t4 >= 1:
                t.wait_ge(L.s("p6bcev"), t4)
            t.matmul(ps[2][:, :], onesr_f[0:1, :], rows_m[0:1, t4 * 512:(t4 + 1) * 512],
                     start=True, stop=True)
            L.inc(t.matmul(ps[3][:, :], onesr_f[0:1, :], rows_r[0:1, t4 * 512:(t4 + 1) * 512],
                           start=True, stop=True), "p6bc", 1)

        @block.sync
        def _(sync, t4=t4):
            if t4 == 0:
                sync.wait_ge(L.s("p6cp"), 128)
            for cc in range(NF):
                i, u = P6SLOT[(t4, cc)]
                if u >= 1:
                    sync.wait_ge(L.s(f"p6ou{i}"), u)
                L.inc(sync.dma_start(out=ffb[i][:],
                                     in_=ffsp[cc, :, t4 * 512:(t4 + 1) * 512]), f"p6lf2{i}", 16)

        @block.vector
        def _(v, t4=t4):
            v.wait_ge(L.s("p6bc"), t4 + 1)
            v.tensor_copy(mb4[:], ps[2][:, :])
            L.inc(v.tensor_copy(rb4[:], ps[3][:, :]), "p6bcev", 1)
            for cc in range(NF):
                i, u = P6SLOT[(t4, cc)]
                v.wait_ge(L.s(f"p6lf2{i}"), 16 * (u + 1))
                v.wait_ge(L.s(f"p6lx{i}"), 16 * (u + 1))
                if u >= 1:
                    v.wait_ge(L.s(f"p6o{i}"), 16 * u)
                v.tensor_tensor(ffb[i][:], ffb[i][:], mb4[:], ALU.subtract)
                v.tensor_tensor(ffb[i][:], ffb[i][:], rb4[:], ALU.mult)
                L.inc(v.tensor_tensor(ob6[i][:], ffb[i][:], x2r[i][:], ALU.add),
                      f"p6ou{i}", 1)

        @block.gpsimd
        def _(g, t4=t4):
            for cc in range(NF):
                k = t4 * NF + cc
                if k == 0:
                    for kk in range(6):
                        _p6issue_x2r(g, kk)
                if k + 6 < 4 * NF:
                    _p6issue_x2r(g, k + 6)

        @block.scalar
        def _(sc, t4=t4):
            for cc in range(NF):
                i, u = P6SLOT[(t4, cc)]
                sc.wait_ge(L.s(f"p6ou{i}"), u + 1)
                L.inc(sc.dma_start(out=outT[cc, :, t4 * 512:(t4 + 1) * 512],
                                   in_=ob6[i][:]), f"p6o{i}", 16)

    @block.gpsimd
    def _(g):
        for i in range(6):
            if f"p6o{i}" in L.d:
                g.wait_ge(L.s(f"p6o{i}"), L.n(f"p6o{i}"))
    barrier()
    if DEBUG_SPILLS:
        @block.sync
        def _(sync):
            L.inc(sync.dma_start(out=dbg_zw, in_=zw[:]), "dbgo", 16)
            L.inc(sync.dma_start(out=dbg_ln, in_=ln[:]), "dbgo", 16)
            L.inc(sync.dma_start(out=dbg_bsel, in_=bsel[:]), "dbgo", 16)
            L.inc(sync.dma_start(out=dbg_ar, in_=ar_slab[:]), "dbgo", 16)
            sync.wait_ge(L.s("dbgo"), L.n("dbgo"))
        barrier()
    p6s.__exit__(None, None, None)

    es.__exit__(None, None, None)
    return nc


def _prep_shared(inputs):
    c = _host_consts()
    sh = {
        "wq": _tile_w(np.ascontiguousarray(inputs["L0_attn_Wq"].T)),
        "wk": _tile_w(np.ascontiguousarray(inputs["L0_attn_Wk"].T)),
        "wv": _tile_w(np.ascontiguousarray(inputs["L0_attn_Wv"].T)),
        "wbs": _tile_w(np.ascontiguousarray(inputs["L0_ffn_W_base"].T)),
        "bq_t": np.ascontiguousarray(inputs["L0_attn_bq"].reshape(NF, 128).T),
        "bk_t": np.ascontiguousarray(inputs["L0_attn_bk"].reshape(NF, 128).T),
        "bb_t": np.ascontiguousarray(inputs["L0_ffn_b_base"].reshape(NF, 128).T),
        "bv_r": np.ascontiguousarray(inputs["L0_attn_bv"].reshape(1, D)),
        "bwT": np.ascontiguousarray(inputs["L0_attn_bias_weight"].T),
        "active": np.ascontiguousarray(inputs["L0_ffn_active"].reshape(1, 4)),
        "prior": np.ascontiguousarray(
            np.concatenate([inputs["L0_ffn_domain_prior"].reshape(16, 1),
                            np.zeros((16, 1), np.float32)])),
        "rb_in": np.ascontiguousarray(
            np.concatenate([inputs["L0_ffn_router_b"].reshape(16, 1),
                            np.zeros((16, 1), np.float32)])),
        "routW": np.ascontiguousarray(
            np.concatenate([inputs["L0_ffn_router_W"],
                            np.zeros((16, D), np.float32)])),
        "loraA": np.ascontiguousarray(inputs["L0_ffn_lora_A"].reshape(128, D)),
        "loraB": np.ascontiguousarray(inputs["L0_ffn_lora_B"].transpose(0, 2, 1).reshape(128, D)),
    }
    sh.update(c)
    return sh


def kernel(**inputs):
    x = inputs["L0_attn_x"]
    sh = _prep_shared(inputs)
    nc = build_program()
    in_maps = []
    for core in range(N_CORES):
        m = dict(sh)
        m["xT"] = np.ascontiguousarray(x[core].T)
        in_maps.append(m)
    res = run_bass_kernel_spmd(nc, in_maps, core_ids=list(range(N_CORES)))
    out = np.empty((N_CORES, T, D), dtype=np.float32)
    for core in range(N_CORES):
        o = res.results[core]["outT"]
        out[core] = np.asarray(o).reshape(D, T).T
    return out


# version 15 (final)
# speedup vs baseline: 1.1929x; 1.1929x over previous
import contextlib
import numpy as np
import ml_dtypes
import concourse.bass as bass
import concourse.mybir as mybir
from concourse.bass_utils import run_bass_kernel_spmd

F32 = mybir.dt.float32
BF16 = mybir.dt.bfloat16
I32 = mybir.dt.int32
AF = mybir.ActivationFunctionType
ALU = mybir.AluOpType
f32 = np.float32
BF = ml_dtypes.bfloat16

T = 2048
TH = 1024
D = 4096
NF = 32
NH = 16
EPS = 1e-5
EXP_SHIFT = -12.0
N_CORES = 8
DEBUG_SPILLS = False


def _build_tables():
    g = [1.0, 1.0, 1.0]
    dim, n = 3, 8
    masks = sorted(range(n), key=lambda m: (bin(m).count('1'), m))
    inv = {m: i for i, m in enumerate(masks)}
    C = np.zeros((n, n, n), dtype=np.float32)
    for i, a in enumerate(masks):
        for j, b in enumerate(masks):
            sw, at = 0, a >> 1
            while at:
                sw += bin(at & b).count('1')
                at >>= 1
            s = -1.0 if sw & 1 else 1.0
            for t in range(dim):
                if a >> t & 1 and b >> t & 1:
                    s *= float(g[t])
            C[i, j, inv[a ^ b]] = s
    grades = np.array([bin(m).count('1') for m in masks])
    rev = np.where(grades * (grades - 1) // 2 % 2 == 1, -1.0, 1.0).astype(np.float32)
    met = np.ones(8, dtype=np.float32)
    return C[:, :, [4, 5, 6]], rev, met


def _host_consts():
    biv, rev, met = _build_tables()
    scale = 1.0 / 16.0
    c = {}
    c["bivk3"] = np.ascontiguousarray(biv.transpose(2, 0, 1).reshape(3, 64)).astype(np.float32)
    c["rev8"] = rev.reshape(8, 1).astype(np.float32)
    c["d0t"] = np.ascontiguousarray(np.tile(scale * np.diag(rev * met), (1, 16))).astype(np.float32)
    p8 = np.zeros((8, 128), dtype=np.float32)
    for d_ in range(16):
        for i in range(8):
            p8[i, d_ * 8 + i] = 1.0
    c["p8"] = p8
    bdm = np.zeros((128, 128), dtype=np.float32)
    for d_ in range(16):
        bdm[d_ * 8:(d_ + 1) * 8, d_ * 8:(d_ + 1) * 8] = 1.0
    c["bdmask"] = bdm
    p16 = np.zeros((32, 128), dtype=np.float32)
    for e in range(16):
        for r in range(8):
            p16[e, e * 8 + r] = 1.0
    c["p16"] = p16
    p4e = np.zeros((32, 32), dtype=np.float32)
    for e in range(4):
        for r in range(8):
            p4e[e, e * 8 + r] = 1.0
    c["p4e"] = p4e
    dg8 = np.zeros((128, 32), dtype=np.float32)
    for e in range(16):
        for rt in range(8):
            for ep in range(4):
                dg8[e * 8 + rt, ep * 8 + rt] = 1.0
    c["diag8"] = dg8
    c["iota16"] = np.arange(32, dtype=np.float32).reshape(32, 1)
    return c


def _tile_w(Wt):
    return np.ascontiguousarray(
        Wt.reshape(NF, 128, 8, 512).transpose(2, 1, 0, 3).reshape(8, 128, NF * 512))


class Ledger:
    def __init__(self, es, nc):
        self.es, self.nc, self.d = es, nc, {}

    def s(self, name):
        if name not in self.d:
            self.d[name] = [self.nc.alloc_semaphore(name=name), 0]
        return self.d[name][0]

    def take(self, prefixes):
        out = []
        for name in list(self.d):
            if any(name.startswith(p) or name == p for p in prefixes):
                out.append((name, self.d[name][0], self.d[name][1]))
                del self.d[name]
        return out

    def inc(self, ins, name, amt):
        h = self.s(name)
        ins.then_inc(h, amt)
        self.d[name][1] += amt
        return self.d[name][1]

    def n(self, name):
        self.s(name)
        return self.d[name][1]


def build_program():
    nc = bass.Bass("TRN2", target_bir_lowering=False, detect_race_conditions=False)

    xT = nc.dram_tensor("xT", [D, T], F32, kind="ExternalInput").ap()
    wq = nc.dram_tensor("wq", [8, 128, NF * 512], F32, kind="ExternalInput").ap()
    wk = nc.dram_tensor("wk", [8, 128, NF * 512], F32, kind="ExternalInput").ap()
    wv = nc.dram_tensor("wv", [8, 128, NF * 512], F32, kind="ExternalInput").ap()
    wbs = nc.dram_tensor("wbs", [8, 128, NF * 512], F32, kind="ExternalInput").ap()
    bq_t = nc.dram_tensor("bq_t", [128, NF], F32, kind="ExternalInput").ap()
    bk_t = nc.dram_tensor("bk_t", [128, NF], F32, kind="ExternalInput").ap()
    bb_t = nc.dram_tensor("bb_t", [128, NF], F32, kind="ExternalInput").ap()
    bv_r = nc.dram_tensor("bv_r", [1, D], F32, kind="ExternalInput").ap()
    bwT = nc.dram_tensor("bwT", [3, 16], F32, kind="ExternalInput").ap()
    active = nc.dram_tensor("active", [1, 4], I32, kind="ExternalInput").ap()
    prior = nc.dram_tensor("prior", [32, 1], F32, kind="ExternalInput").ap()
    rb_in = nc.dram_tensor("rb_in", [32, 1], F32, kind="ExternalInput").ap()
    routW = nc.dram_tensor("routW", [32, D], F32, kind="ExternalInput").ap()
    loraA = nc.dram_tensor("loraA", [128, D], F32, kind="ExternalInput").ap()
    loraB = nc.dram_tensor("loraB", [128, D], F32, kind="ExternalInput").ap()
    bivk3 = nc.dram_tensor("bivk3", [3, 64], F32, kind="ExternalInput").ap()
    rev8 = nc.dram_tensor("rev8", [8, 1], F32, kind="ExternalInput").ap()
    d0t = nc.dram_tensor("d0t", [8, 128], F32, kind="ExternalInput").ap()
    p8 = nc.dram_tensor("p8", [8, 128], F32, kind="ExternalInput").ap()
    bdmask = nc.dram_tensor("bdmask", [128, 128], F32, kind="ExternalInput").ap()
    p16 = nc.dram_tensor("p16", [32, 128], F32, kind="ExternalInput").ap()
    p4e = nc.dram_tensor("p4e", [32, 32], F32, kind="ExternalInput").ap()
    diag8 = nc.dram_tensor("diag8", [128, 32], F32, kind="ExternalInput").ap()
    iota16 = nc.dram_tensor("iota16", [32, 1], F32, kind="ExternalInput").ap()

    outT = nc.dram_tensor("outT", [NF, 128, T], F32, kind="ExternalOutput").ap()

    spk = dict(kind="ExternalOutput") if DEBUG_SPILLS else {}
    qsp = nc.dram_tensor("qsp", [NF, 128, T], BF16, **spk).ap()
    ksp = nc.dram_tensor("ksp", [NF, 128, T], BF16, **spk).ap()
    vsp = nc.dram_tensor("vsp", [NH, 128, 4096], BF16, **spk).ap()
    aosp = nc.dram_tensor("aosp", [NF, 128, T], F32, **spk).ap()
    x2sp = nc.dram_tensor("x2sp", [NF, 128, T], F32, **spk).ap()
    ffsp = nc.dram_tensor("ffsp", [NF, 128, T], F32, **spk).ap()
    wk64 = nc.dram_tensor("wk64", [16, 64], F32).ap()
    if DEBUG_SPILLS:
        dbg_zw = nc.dram_tensor("dbg_zw", [32, T], BF16, kind="ExternalOutput").ap()
        dbg_ln = nc.dram_tensor("dbg_ln", [128, NF * TH], BF16, kind="ExternalOutput").ap()
        dbg_bsel = nc.dram_tensor("dbg_bsel", [32, D], BF16, kind="ExternalOutput").ap()
        dbg_ar = nc.dram_tensor("dbg_ar", [128, NF * 36], BF16, kind="ExternalOutput").ap()
        dbg_rwb = nc.dram_tensor("dbg_rwb", [32, D], BF16, kind="ExternalOutput").ap()
        dbg_stb = nc.dram_tensor("dbg_stb", [32, 4], BF16, kind="ExternalOutput").ap()

    es = contextlib.ExitStack()
    es.__enter__()
    L = Ledger(es, nc)

    def sb(name, shape, dt, stack=None):
        return (stack or es).enter_context(nc.sbuf_tensor(name, shape, dt))

    ln = sb("ln", [128, NF * TH], BF16)
    bqc = sb("bqc", [128, NF], F32)
    bkc = sb("bkc", [128, NF], F32)
    bbc = sb("bbc", [128, NF], F32)
    bvb = sb("bvb", [1, D], BF16)
    bd_all = sb("bd_all", [128, NH * 128], BF16)
    ar_slab = sb("ar_slab", [128, NF * 36], BF16)
    bsel = sb("bsel", [32, D], BF16)
    zw = sb("zw", [32, T], BF16)
    logp = sb("logp", [4, 1], F32)
    rbsel = sb("rbsel", [4, 1], F32)
    ones_bf = sb("ones_bf", [128, 1], BF16)
    onesr_bf = sb("onesr_bf", [1, 128], BF16)
    onesr_f = sb("onesr_f", [1, 128], F32)
    ones16_f = sb("ones16_f", [1, 32], F32)
    ones4c_f = sb("ones4c_f", [32, 1], F32)
    zeroc = sb("zeroc", [128, 1], F32)
    p4e_c = sb("p4e_c", [32, 32], F32)
    ps = [es.enter_context(nc.psum_tensor(f"ps{i}", [128, 512], F32)) for i in range(8)]

    block = es.enter_context(nc.Block())
    BAR = L.s("bar")
    bar_n = [0]

    def barrier(free=None):
        target = bar_n[0] + 5
        freed = L.take(free) if free else []
        for eng in (block.sync, block.scalar, block.vector, block.tensor):
            @eng
            def _(e):
                e.nop().then_inc(BAR, 1)
                e.wait_ge(BAR, target)

        @block.gpsimd
        def _(g):
            for name, h, n in freed:
                if n:
                    g.wait_ge(h, n)
            if freed:
                nc.clear_and_free_semaphores([h for _, h, _ in freed])
            g.nop().then_inc(BAR, 1)
            g.wait_ge(BAR, target)
        bar_n[0] = target

    p0s = contextlib.ExitStack()
    p0s.__enter__()
    bw3 = sb("bw3", [3, 16], F32, p0s)
    act_i = sb("act_i", [1, 4], I32, p0s)
    act_f = sb("act_f", [1, 4], F32, p0s)
    prior_s = sb("prior_s", [32, 1], F32, p0s)
    rb_s = sb("rb_s", [32, 1], F32, p0s)
    routw_b = sb("routw_b", [32, D], BF16, p0s)
    lA_b = sb("lA_b", [128, D], BF16, p0s)
    lB_b = sb("lB_b", [128, D], BF16, p0s)
    bivk_s = sb("bivk_s", [3, 64], F32, p0s)
    rev_s = sb("rev_s", [8, 1], F32, p0s)
    d0_s = sb("d0_s", [8, 128], F32, p0s)
    p8_s = sb("p8_s", [8, 128], F32, p0s)
    bdm_s = sb("bdm_s", [128, 128], F32, p0s)
    p16_s = sb("p16_s", [32, 128], F32, p0s)
    dg8_s = sb("dg8_s", [128, 32], F32, p0s)
    io16_s = sb("io16_s", [32, 1], F32, p0s)
    st_f = sb("st_f", [32, 4], F32, p0s)
    st_f2 = sb("st_f2", [32, 4], F32, p0s)
    st_b = sb("st_b", [32, 4], BF16, p0s)
    wk_sb = sb("wk_sb", [16, 64], F32, p0s)
    mall = sb("mall", [8, 128], F32, p0s)
    ms2 = sb("ms2", [8, 128], F32, p0s)
    t1_sb = sb("t1_sb", [8, 128], F32, p0s)
    t1s_sb = sb("t1s_sb", [32, 128], F32, p0s)
    sel_b = sb("sel_b", [128, 32], BF16, p0s)

    @block.sync
    def _(sync):
        loads = [(bqc[:], bq_t), (bkc[:], bk_t), (bbc[:], bb_t), (p4e_c[:], p4e),
                 (bw3[:], bwT), (act_i[:], active), (prior_s[:], prior),
                 (rb_s[:], rb_in),
                 (bivk_s[:], bivk3), (rev_s[:], rev8), (d0_s[:], d0t), (p8_s[:], p8),
                 (bdm_s[:], bdmask), (p16_s[:], p16), (dg8_s[:], diag8), (io16_s[:], iota16)]
        for dst, src in loads:
            L.inc(sync.dma_start(out=dst, in_=src), "p0ld", 16)

    @block.gpsimd
    def _(g):
        L.inc(g.dma_start(out=bvb[:], in_=bv_r), "p0ldg", 16)
        L.inc(g.dma_start(out=routw_b[:], in_=routW), "p0ldg", 16)
        L.inc(g.dma_start(out=lA_b[:], in_=loraA), "p0ldg", 16)
        L.inc(g.dma_start(out=lB_b[:], in_=loraB), "p0ldg", 16)

    @block.vector
    def _(v):
        v.wait_ge(L.s("p0ld"), L.n("p0ld"))
        for ms in (v.memset(ones_bf[:], 1.0), v.memset(onesr_bf[:], 1.0),
                   v.memset(onesr_f[:], 1.0), v.memset(ones16_f[:], 1.0),
                   v.memset(ones4c_f[:], 1.0), v.memset(t1s_sb[:], 0.0),
                   v.memset(zeroc[:], 0.0)):
            L.inc(ms, "p0ms", 1)
        L.inc(v.tensor_copy(act_f[:], act_i[:]), "p0cast", 1)

    nc.const_aps.register(F32, 0.0, zeroc[:, 0:1])

    @block.tensor
    def _(t):
        t.wait_ge(L.s("p0ms"), 7)
        t.wait_ge(L.s("p0cast"), 1)
        t.matmul(ps[0][0:32, 0:4], ones16_f[0:1, :], act_f[0:1, :], start=True, stop=True)
        L.inc(t.matmul(ps[1][0:16, 0:64], bw3[:, :], bivk_s[:, :], start=True, stop=True),
              "p0mm", 1)

    @block.vector
    def _(v):
        v.wait_ge(L.s("p0mm"), 1)
        L.inc(v.tensor_scalar(st_f[:], ps[0][0:32, 0:4], io16_s[:, 0:1], None,
                              ALU.is_equal), "p0ieq", 1)
        L.inc(v.tensor_copy(wk_sb[:], ps[1][0:16, 0:64]), "p0ev", 1)

    @block.scalar
    def _(sc):
        sc.wait_ge(L.s("p0ieq"), 1)
        L.inc(sc.copy(st_b[:], st_f[:]), "p0stb", 1)
        L.inc(sc.copy(st_f2[:], st_f[:]), "p0stf", 1)

    @block.sync
    def _(sync):
        sync.wait_ge(L.s("p0ev"), 1)
        L.inc(sync.dma_start(out=wk64, in_=wk_sb[:]), "p0wka", 16)
        sync.wait_ge(L.s("p0wka"), 16)
        L.inc(sync.dma_start(out=mall[:, :].rearrange("i (h j) -> i h j", h=16),
                             in_=wk64.rearrange("h (i j) -> i h j", i=8)),
              "p0wkb", 16)

    @block.vector
    def _(v):
        v.wait_ge(L.s("p0wkb"), 16)
        v.tensor_scalar(ms2[:], mall[:], rev_s[:, 0:1], None, ALU.mult)
        L.inc(v.tensor_tensor(ms2[:], ms2[:], d0_s[:], ALU.add), "p0m2", 1)

    for h in range(NH):
        @block.tensor
        def _(t, h=h):
            if h == 0:
                t.wait_ge(L.s("p0m2"), 1)
            else:
                t.wait_ge(L.s("p0t1ev"), h)
            L.inc(t.matmul(ps[2][0:8, 0:128], ms2[:, h * 8:(h + 1) * 8], p8_s[:, :],
                           start=True, stop=True), "p0t1", 1)
            t.wait_ge(L.s("p0t1ev"), h + 1)
            if h >= 1:
                t.wait_ge(L.s("p0bd_ev"), h)
            L.inc(t.matmul(ps[3][0:128, 0:128], t1_sb[:, :], p8_s[:, :],
                           start=True, stop=True), "p0t2", 1)

        @block.vector
        def _(v, h=h):
            v.wait_ge(L.s("p0t1"), h + 1)
            L.inc(v.tensor_copy(t1_sb[:], ps[2][0:8, 0:128]), "p0t1ev", 1)
            v.wait_ge(L.s("p0t2"), h + 1)
            L.inc(v.tensor_tensor(bd_all[:, h * 128:(h + 1) * 128], ps[3][0:128, 0:128],
                                  bdm_s[:], ALU.mult), "p0bd_ev", 1)

    @block.tensor
    def _(t):
        t.wait_ge(L.s("p0stf"), 1)
        L.inc(t.matmul(ps[4][0:4, 0:128], st_f2[:, :], p16_s[:, :], start=True, stop=True),
              "p0s1", 1)
        t.wait_ge(L.s("p0s1ev"), 1)
        L.inc(t.matmul(ps[5][0:128, 0:32], t1s_sb[:, :], p4e_c[:, :],
                       start=True, stop=True), "p0s2", 1)
        t.matmul(ps[6][0:4, 0:1], st_f2[:, :], prior_s[:, :], start=True, stop=True)
        L.inc(t.matmul(ps[6][0:4, 1:2], st_f2[:, :], rb_s[:, :], start=True, stop=True),
              "p0pr", 1)

    @block.vector
    def _(v):
        v.wait_ge(L.s("p0s1"), 1)
        L.inc(v.tensor_copy(t1s_sb[0:4, :], ps[4][0:4, 0:128]), "p0s1ev", 1)
        v.wait_ge(L.s("p0s2"), 1)
        L.inc(v.tensor_tensor(sel_b[:], ps[5][0:128, 0:32], dg8_s[:], ALU.mult), "p0sel", 1)
        v.wait_ge(L.s("p0pr"), 1)
        L.inc(v.tensor_copy(rbsel[:], ps[6][0:4, 1:2]), "p0rb", 1)

    @block.scalar
    def _(sc):
        sc.wait_ge(L.s("p0pr"), 1)
        L.inc(sc.activation(logp[:], ps[6][0:4, 0:1], AF.Ln, bias=f32(1e-12)), "p0lp", 1)

    @block.tensor
    def _(t):
        t.wait_ge(L.s("p0sel"), 1)
        t.wait_ge(L.s("p0ldg"), 64)
        t.wait_ge(L.s("p0stb"), 1)
        for ob in range(8):
            if ob >= 1:
                t.wait_ge(L.s("p0bs_ev"), ob)
            L.inc(t.matmul(ps[7][0:32, :], sel_b[:, :], lB_b[:, ob * 512:(ob + 1) * 512],
                           start=True, stop=True), "p0bs", 1)
        t.wait_ge(L.s("p0bd_ev"), NH)
        t.wait_ge(L.s("p0ev"), 1)
        for ft in range(NF):
            b0 = (ft % 2) * 2
            if ft >= 2:
                t.wait_ge(L.s("p0ar_ev"), 2 * (ft - 1))
            L.inc(t.matmul(ps[b0][0:128, 0:32], lA_b[:, ft * 128:(ft + 1) * 128],
                           sel_b[:, :], start=True, stop=True), "p0ar", 1)
            L.inc(t.matmul(ps[b0 + 1][0:128, 0:4], routw_b[:, ft * 128:(ft + 1) * 128],
                           st_b[:, :], start=True, stop=True), "p0ar", 1)

    @block.vector
    def _(v):
        for ob in range(8):
            v.wait_ge(L.s("p0bs"), ob + 1)
            L.inc(v.tensor_scalar(bsel[:, ob * 512:(ob + 1) * 512], ps[7][0:32, :],
                                  2.0, None, ALU.mult), "p0bs_ev", 1)
        for ft in range(NF):
            b0 = (ft % 2) * 2
            v.wait_ge(L.s("p0ar"), 2 * ft + 2)
            L.inc(v.tensor_copy(ar_slab[:, ft * 36:ft * 36 + 32], ps[b0][0:128, 0:32]),
                  "p0ar_ev", 1)
            L.inc(v.tensor_copy(ar_slab[:, ft * 36 + 32:ft * 36 + 36],
                                ps[b0 + 1][0:128, 0:4]), "p0ar_ev", 1)

    @block.gpsimd
    def _(g):
        for s in ("p0bd_ev", "p0ar_ev", "p0bs_ev", "p0lp", "p0rb", "p0sel"):
            g.wait_ge(L.s(s), L.n(s))
        if DEBUG_SPILLS:
            L.inc(g.dma_start(out=dbg_rwb, in_=routw_b[:]), "p0dbg", 16)
            L.inc(g.dma_start(out=dbg_stb, in_=st_b[:]), "p0dbg", 16)
            g.wait_ge(L.s("p0dbg"), L.n("p0dbg"))
    barrier(free=["p0"])
    p0s.__exit__(None, None, None)

    p12 = contextlib.ExitStack()
    p12.__enter__()
    xbufA = [sb(f"xbufA{i}", [128, 512], F32, p12) for i in range(4)]
    xbufB = [sb(f"xbufB{i}", [128, 512], F32, p12) for i in range(5)]
    xb_bf = [sb(f"xb_bf{i}", [128, 512], BF16, p12) for i in range(4)]
    sq_bf = [sb(f"sq_bf{i}", [128, 512], BF16, p12) for i in range(4)]
    mrow = sb("mrow", [1, 512], F32, p12)
    rrow = sb("rrow", [1, 512], F32, p12)
    vrow_ = sb("vrow_", [1, 512], F32, p12)
    mbc = sb("mbc", [128, 512], F32, p12)
    rbc = sb("rbc", [128, 512], F32, p12)
    slab = [sb(f"slab{i}", [128, NF * 512], BF16, p12) for i in range(2)]
    evq = [sb(f"evq{i}", [128, 1024], BF16, p12) for i in range(4)]
    evv = [sb(f"evv{i}", [128, 2048], BF16, p12) for i in range(2)]
    P1KD, P1KP, P1SLOT = [0], [0], {}

    for H in range(2):
        for tt in range(2):
            t4 = H * 2 + tt
            gt = t4

            @block.sync
            def _(sync, t4=t4, gt=gt):
                for cc in range(NF):
                    k = gt * NF + cc
                    if k % 2 == 1 and k >= 5:
                        continue
                    i = k % 4
                    if k >= 4:
                        sync.wait_ge(L.s("p1cp"), k - 3)
                    L.inc(sync.dma_start(
                        out=xbufA[i][:],
                        in_=xT[cc * 128:(cc + 1) * 128, t4 * 512:(t4 + 1) * 512]),
                        f"p1lA{i}", 16)

            @block.scalar
            def _(sc, gt=gt):
                for cc in range(NF):
                    k = gt * NF + cc
                    i = k % 4
                    sc.wait_ge(L.s(f"p1lA{i}"), 16 * (k // 4 + 1))
                    if k >= 4:
                        sc.wait_ge(L.s("p1mm"), 2 * (k - 3))
                    L.inc(sc.copy(xb_bf[k % 4][:], xbufA[i][:]), "p1cp", 1)
                    kk = k + 4
                    if kk % 2 == 1 and 5 <= kk < 4 * NF:
                        t4b, ccb = divmod(kk, NF)
                        L.inc(sc.dma_start(
                            out=xbufA[kk % 4][:],
                            in_=xT[ccb * 128:(ccb + 1) * 128, t4b * 512:(t4b + 1) * 512]),
                            f"p1lA{kk % 4}", 16)

            @block.vector
            def _(v, gt=gt):
                for cc in range(NF):
                    i = cc % 4
                    k = gt * NF + cc
                    v.wait_ge(L.s("p1cp"), k + 1)
                    if k >= 4:
                        v.wait_ge(L.s("p1mm"), 2 * (k - 3))
                    L.inc(v.tensor_tensor(sq_bf[k % 4][:], xb_bf[k % 4][:],
                                          xb_bf[k % 4][:], ALU.mult), "p1sqv", 1)

            @block.tensor
            def _(t, gt=gt):
                for cc in range(NF):
                    k = gt * NF + cc
                    t.wait_ge(L.s("p1sqv"), k + 1)
                    t.matmul(ps[0][0:1, :], ones_bf[:, 0:1], xb_bf[k % 4][:],
                             start=(cc == 0), stop=(cc == NF - 1))
                    L.inc(t.matmul(ps[1][0:1, :], ones_bf[:, 0:1], sq_bf[k % 4][:],
                                   start=(cc == 0), stop=(cc == NF - 1)), "p1mm", 2)
                t.wait_ge(L.s("p1row"), gt + 1)
                t.wait_ge(L.s("p1mr"), gt + 1)
                if gt >= 1:
                    t.wait_ge(L.s("p1bcev"), gt)
                t.matmul(ps[2][:, :], onesr_f[0:1, :], mrow[0:1, :], start=True, stop=True)
                L.inc(t.matmul(ps[3][:, :], onesr_f[0:1, :], rrow[0:1, :],
                               start=True, stop=True), "p1bc", 1)

            @block.vector
            def _(v, gt=gt):
                v.wait_ge(L.s("p1mm"), 2 * NF * (gt + 1))
                L.inc(v.tensor_scalar(mrow[0:1, :], ps[0][0:1, :], 1.0 / D, None, ALU.mult),
                      "p1mr", 1)
                v.tensor_scalar(vrow_[0:1, :], ps[1][0:1, :], 1.0 / D, None, ALU.mult)
                v.tensor_tensor(rrow[0:1, :], mrow[0:1, :], mrow[0:1, :], ALU.mult)
                L.inc(v.tensor_tensor(vrow_[0:1, :], vrow_[0:1, :], rrow[0:1, :],
                                      ALU.subtract), "p1var", 1)

            @block.scalar
            def _(sc, gt=gt):
                sc.wait_ge(L.s("p1var"), gt + 1)
                sc.activation(rrow[0:1, :], vrow_[0:1, :], AF.Ln, bias=f32(EPS))
                L.inc(sc.activation(rrow[0:1, :], rrow[0:1, :], AF.Exp, scale=f32(-0.5)),
                      "p1row", 1)

            @block.scalar
            def _(sc, t4=t4, gt=gt):
                for cc in range(NF):
                    i = P1KD[0] % 5
                    u = P1KD[0] // 5
                    P1KD[0] += 1
                    P1SLOT[(gt, cc)] = (i, u)
                    if u >= 1:
                        sc.wait_ge(L.s(f"p1cB{i}"), u)
                    L.inc(sc.dma_start(
                        out=xbufB[i][:],
                        in_=xT[cc * 128:(cc + 1) * 128, t4 * 512:(t4 + 1) * 512]),
                        f"p1lB{i}", 16)

            @block.vector
            def _(v, tt=tt, gt=gt):
                v.wait_ge(L.s("p1bc"), gt + 1)
                v.tensor_copy(mbc[:], ps[2][:, :])
                L.inc(v.tensor_copy(rbc[:], ps[3][:, :]), "p1bcev", 1)
                for cc in range(NF):
                    i, u = P1SLOT[(gt, cc)]
                    v.wait_ge(L.s(f"p1lB{i}"), 16 * (u + 1))
                    v.tensor_tensor(xbufB[i][:], xbufB[i][:], mbc[:], ALU.subtract)
                    L.inc(v.tensor_tensor(
                        ln[:, cc * TH + tt * 512: cc * TH + (tt + 1) * 512],
                        xbufB[i][:], rbc[:], ALU.mult), f"p1cB{i}", 1)

        @block.gpsimd
        def _(g):
            for i in range(6):
                if f"p1cB{i}" in L.d:
                    g.wait_ge(L.s(f"p1cB{i}"), L.n(f"p1cB{i}"))
        barrier()

        weights = [("q", wq, qsp, bqc), ("k", wk, ksp, bkc), ("v", wv, vsp, None)]
        if H == 0:
            SLAB_MM = {}
        for wi, (wname, wsrc, wspill, bias) in enumerate(weights):
            for b in range(8):
                sidx = wi * 8 + b
                gs = H * 24 + sidx
                par = gs % 2

                @block.gpsimd
                def _(g, b=b, gs=gs, par=par, wsrc=wsrc):
                    if gs >= 2 and (gs - 2) in SLAB_MM:
                        g.wait_ge(L.s("p2mm"), SLAB_MM[gs - 2])
                    L.inc(g.dma_start(out=slab[par][:], in_=wsrc[b]), f"p2wl{par}", 16)

                if wname == "q":
                    def emit_q_sub(sw, b=b, gs=gs, par=par):
                        sub = sw
                        cc = b * 4 + sub

                        @block.tensor
                        def _(t, sw=sw, sub=sub, par=par, gs=gs):
                            t.wait_ge(L.s("p2evq"), max(0, L.n("p2evq") - 2))
                            t.wait_ge(L.s("p2evv"), L.n("p2evv"))
                            if sw == 0:
                                t.wait_ge(L.s(f"p2wl{par}"), 16 * (gs // 2 + 1))
                                t.wait_ge(L.s("p2bde"), L.n("p2bde"))
                            for fi in range(NF):
                                lhsT = slab[par][:, fi * 512 + sub * 128: fi * 512 + sub * 128 + 128]
                                for tt in range(2):
                                    ins = t.matmul(
                                        ps[(sw % 2) * 2 + tt][:, :], lhsT,
                                        ln[:, fi * TH + tt * 512: fi * TH + (tt + 1) * 512],
                                        start=(fi == 0), stop=(fi == NF - 1))
                            L.inc(ins, "p2mm", 1)

                        @block.vector
                        def _(v, sw=sw, sub=sub, cc=cc):
                            v.wait_ge(L.s("p2mm"), L.n("p2mm"))
                            nsp = L.n(f"p2sq{sub}")
                            if nsp:
                                v.wait_ge(L.s(f"p2sq{sub}"), nsp)
                            for tt in range(2):
                                L.inc(v.tensor_scalar(evq[sub][:, tt * 512:(tt + 1) * 512],
                                                      ps[(sw % 2) * 2 + tt][:, :],
                                                      bqc[:, cc:cc + 1], None, ALU.add),
                                      "p2evq", 1)

                    def emit_q_bd(sw, b=b, H=H):
                        sub = sw
                        cc = b * 4 + sub
                        hbd = cc // 2

                        @block.tensor
                        def _(t, sub=sub, hbd=hbd):
                            t.wait_ge(L.s("p2evq"), L.n("p2evq"))
                            t.wait_ge(L.s("p2bde"), L.n("p2bde"))
                            for half in range(2):
                                L.inc(t.matmul(ps[4 + half][:, :],
                                               bd_all[:, hbd * 128:(hbd + 1) * 128],
                                               evq[sub][:, half * 512:(half + 1) * 512],
                                               start=True, stop=True), "p2bd", 1)

                        @block.vector
                        def _(v, sub=sub):
                            v.wait_ge(L.s("p2bd"), L.n("p2bd"))
                            for half in range(2):
                                L.inc(v.tensor_copy(evq[sub][:, half * 512:(half + 1) * 512],
                                                    ps[4 + half][:, :]), "p2bde", 1)

                        @block.scalar
                        def _(sc, sub=sub, cc=cc, H=H):
                            sc.wait_ge(L.s("p2bde"), L.n("p2bde"))
                            L.inc(sc.dma_start(out=qsp[cc, :, H * TH:(H + 1) * TH],
                                               in_=evq[sub][:]), f"p2sq{sub}", 16)

                    for sw in range(4):
                        if sw >= 2:
                            emit_q_bd(sw - 2)
                        emit_q_sub(sw)
                    emit_q_bd(2)
                    emit_q_bd(3)
                    SLAB_MM[gs] = L.n("p2mm")
                    continue

                @block.tensor
                def _(t, wname=wname, b=b, gs=gs, par=par):
                    t.wait_ge(L.s(f"p2wl{par}"), 16 * (gs // 2 + 1))
                    if wname == "k":
                        for hw in range(2):
                            if gs >= 1:
                                t.wait_ge(L.s("p2evq"),
                                          max(0, L.n("p2evq") - (4 if hw == 0 else 0)))
                                t.wait_ge(L.s("p2evv"), L.n("p2evv"))
                                if hw == 1:
                                    t.wait_ge(L.s("p2bde"), L.n("p2bde"))
                            for fi in range(NF):
                                for sub in (2 * hw, 2 * hw + 1):
                                    lhsT = slab[par][:, fi * 512 + sub * 128: fi * 512 + sub * 128 + 128]
                                    for tt in range(2):
                                        ins = t.matmul(
                                            ps[sub * 2 + tt][:, :], lhsT,
                                            ln[:, fi * TH + tt * 512: fi * TH + (tt + 1) * 512],
                                            start=(fi == 0), stop=(fi == NF - 1))
                            L.inc(ins, "p2mm", 1)
                    else:
                        for hw in range(2):
                            if gs >= 1:
                                t.wait_ge(L.s("p2evv"),
                                          max(0, L.n("p2evv") - (8 if hw == 0 else 0)))
                                t.wait_ge(L.s("p2evq"), L.n("p2evq"))
                                if hw == 1:
                                    t.wait_ge(L.s("p2bde"), L.n("p2bde"))
                            for fi in range(NF):
                                for ti in range(4 * hw, 4 * hw + 4):
                                    t.matmul(ps[ti][:, :],
                                             ln[:, fi * TH + ti * 128: fi * TH + ti * 128 + 128],
                                             slab[par][:, fi * 512:(fi + 1) * 512],
                                             start=(fi == 0), stop=False)
                            for ti in range(4 * hw, 4 * hw + 4):
                                ins = t.matmul(ps[ti][:, :], onesr_bf[0:1, 0:128],
                                               bvb[0:1, b * 512:(b + 1) * 512],
                                               start=False, stop=True)
                            L.inc(ins, "p2mm", 1)

                evq_base = L.n("p2evq")
                evv_base = L.n("p2evv")

                @block.vector
                def _(v, wname=wname, b=b, gs=gs):
                    if wname == "k":
                        for hw in range(2):
                            v.wait_ge(L.s("p2mm"), L.n("p2mm") - (1 if hw == 0 else 0))
                            for sub in (2 * hw, 2 * hw + 1):
                                cc = b * 4 + sub
                                nsp = L.n(f"p2sq{sub}")
                                if nsp:
                                    v.wait_ge(L.s(f"p2sq{sub}"), nsp)
                                for tt in range(2):
                                    L.inc(v.tensor_scalar(evq[sub][:, tt * 512:(tt + 1) * 512],
                                                          ps[sub * 2 + tt][:, :],
                                                          bias[:, cc:cc + 1], None, ALU.add),
                                          "p2evq", 1)
                    else:
                        sl0, sl1 = 0, 1
                        for hw in range(2):
                            v.wait_ge(L.s("p2mm"), L.n("p2mm") - (1 if hw == 0 else 0))
                            if hw == 0:
                                for sl in (sl0, sl1):
                                    nsp = L.n(f"p2sv{sl}")
                                    if nsp:
                                        v.wait_ge(L.s(f"p2sv{sl}"), nsp)
                            for ti in range(4 * hw, 4 * hw + 4):
                                L.inc(v.tensor_copy(evv[sl0][:, ti * 256:(ti + 1) * 256],
                                                    ps[ti][:, 0:256]), "p2evv", 1)
                                L.inc(v.tensor_copy(evv[sl1][:, ti * 256:(ti + 1) * 256],
                                                    ps[ti][:, 256:512]), "p2evv", 1)

                @block.scalar
                def _(sc, wname=wname, b=b, H=H, wspill=wspill,
                      evq_base=evq_base, evv_base=evv_base):
                    if wname == "k":
                        for sub in range(4):
                            cc = b * 4 + sub
                            sc.wait_ge(L.s("p2evq"), evq_base + 2 * (sub + 1))
                            L.inc(sc.dma_start(out=wspill[cc, :, H * TH:(H + 1) * TH],
                                               in_=evq[sub][:]), f"p2sq{sub}", 16)
                    else:
                        h0, h1 = 2 * b, 2 * b + 1
                        sl0, sl1 = 0, 1
                        sc.wait_ge(L.s("p2evv"), evv_base + 16)
                        L.inc(sc.dma_start(out=vsp[h0, :, H * 2048:(H + 1) * 2048],
                                           in_=evv[sl0][:]), f"p2sv{sl0}", 16)
                        L.inc(sc.dma_start(out=vsp[h1, :, H * 2048:(H + 1) * 2048],
                                           in_=evv[sl1][:]), f"p2sv{sl1}", 16)

                SLAB_MM[gs] = L.n("p2mm")

        @block.gpsimd
        def _(g):
            for sname in ("p2sq0", "p2sq1", "p2sq2", "p2sq3", "p2sv0", "p2sv1", "p2sv2", "p2sv3"):
                if sname in L.d:
                    g.wait_ge(L.s(sname), L.n(sname))
        barrier(free=(["p1", "p2"] if H == 1 else None))

    p12.__exit__(None, None, None)

    rowsA = sb("rowsA", [128, T], F32)
    rowsB = sb("rowsB", [128, T], F32)
    rows_m = rowsA[0:1, :]
    rows_r = rowsB[0:1, :]
    st2_s = rowsA[32:33, :]
    st2_q = rowsA[64:65, :]
    st3_s = rowsB[32:33, :]
    st3_q = rowsB[64:65, :]

    p3s = contextlib.ExitStack()
    p3s.__enter__()
    qt = [[sb(f"qt{p}_{c}", [128, T], BF16, p3s) for c in range(2)] for p in range(2)]
    kt = [[sb(f"kt{p}_{c}", [128, T], BF16, p3s) for c in range(2)] for p in range(2)]
    vt = [sb(f"vt{p}", [128, 4096], BF16, p3s) for p in range(2)]
    atile = [sb(f"atile{i}", [128, 512], BF16, p3s) for i in range(6)]
    n_sb = sb("n_sb", [1, 512], F32, p3s)
    nb_sb = sb("nb_sb", [128, 512], F32, p3s)
    otile = [sb(f"otile{i}", [128, 512], F32, p3s) for i in range(4)]
    obf = [sb(f"obf{i}", [128, 512], BF16, p3s) for i in range(2)]
    sqf = [sb(f"sqf{i}", [128, 512], BF16, p3s) for i in range(2)]

    @block.vector
    def _(v):
        L.inc(v.memset(st2_s[0:1, :], 0.0), "p3z0", 1)
        L.inc(v.memset(st2_q[0:1, :], 0.0), "p3z0", 1)

    for h in range(NH):
        p = h % 2

        @block.sync
        def _(sync, h=h, p=p):
            if h >= 2:
                sync.wait_ge(L.s("p3s"), 64 * (h - 1))
                sync.wait_ge(L.s("p3av"), 64 * (h - 1))
            for c in range(2):
                L.inc(sync.dma_start(out=qt[p][c][:], in_=qsp[2 * h + c]), f"p3ld{p}", 16)
                L.inc(sync.dma_start(out=kt[p][c][:], in_=ksp[2 * h + c]), f"p3ld{p}", 16)
            L.inc(sync.dma_start(out=vt[p][:], in_=vsp[h]), f"p3ld{p}", 16)

        @block.tensor
        def _(t, h=h, p=p):
            t.wait_ge(L.s(f"p3ld{p}"), 16 * 5 * (h // 2 + 1))
            SB = (0, 1, 5)

            def emit_S(t, tq, tc):
                gq = h * 4 + tq
                gidx = gq * 16 + tc
                if gidx >= 3:
                    t.wait_ge(L.s("p3e"), gidx - 2)
                for c in range(2):
                    sins = t.matmul(ps[SB[gidx % 3]][:, :],
                                    kt[p][c][:, tc * 128:(tc + 1) * 128],
                                    qt[p][c][:, tq * 512:(tq + 1) * 512],
                                    start=(c == 0), stop=(c == 1))
                L.inc(sins, "p3s", 1)

            def emit_nb(t, gq):
                t.wait_ge(L.s("p3rec"), gq + 1)
                t.wait_ge(L.s("p3sta"), gq)
                L.inc(t.matmul(ps[6][:, :], onesr_f[0:1, :], n_sb[0:1, :],
                               start=True, stop=True), "p3nb", 1)

            def emit_stats(t, gq):
                t.wait_ge(L.s("p3nbe"), gq + 1)
                for sub in range(2):
                    t.wait_ge(L.s("p3sq2"), 2 * gq + sub + 1)
                    t.matmul(ps[6][0:1, :], ones_bf[:, 0:1], obf[sub][:],
                             start=(sub == 0), stop=(sub == 1))
                    L.inc(t.matmul(ps[7][0:1, :], ones_bf[:, 0:1], sqf[sub][:],
                                   start=(sub == 0), stop=(sub == 1)), "p3stm", 1)

            for tq in range(4):
                gq = h * 4 + tq
                emit_S(t, tq, 0)
                emit_S(t, tq, 1)
                if tq >= 1:
                    emit_nb(t, gq - 1)
                emit_S(t, tq, 2)
                emit_S(t, tq, 3)
                for step in range(4, 20):
                    if step <= 15:
                        emit_S(t, tq, step)
                    if step == 8 and tq >= 1:
                        emit_stats(t, gq - 1)
                    tc = step - 4
                    gidx = gq * 16 + tc
                    if tc % 2 == 0:
                        t.wait_ge(L.s("p3e"), gidx + 2)
                    if tc == 0:
                        t.wait_ge(L.s("p3rec"), gq)
                        t.wait_ge(L.s("p3oev"), 2 * gq)
                    rs = t.matmul(ps[2][0:1, :], ones_bf[:, 0:1], atile[gidx % 6][:],
                                  start=(tc == 0), stop=(tc == 15))
                    if tc == 15:
                        L.inc(rs, "p3ns", 1)
                    for sub in range(2):
                        ains = t.matmul(ps[3 + sub][:, :],
                                        vt[p][:, tc * 256 + sub * 128: tc * 256 + sub * 128 + 128],
                                        atile[gidx % 6][:],
                                        start=(tc == 0), stop=(tc == 15))
                    L.inc(ains, "p3av", 1)
            emit_nb(t, h * 4 + 3)
            emit_stats(t, h * 4 + 3)

        @block.scalar
        def _(sc, h=h):
            for tq in range(4):
                gq = h * 4 + tq
                for tc in range(16):
                    gidx = gq * 16 + tc
                    sc.wait_ge(L.s("p3s"), gidx + 1)
                    if gidx >= 6:
                        sc.wait_ge(L.s("p3av"), gidx - 5)
                    L.inc(sc.activation(atile[gidx % 6][:], ps[(0, 1, 5)[gidx % 3]][:, :],
                                        AF.Exp, bias=f32(EXP_SHIFT)), "p3e", 1)
                sc.wait_ge(L.s("p3ns"), gq + 1)
                if gq >= 1:
                    sc.wait_ge(L.s("p3nb"), gq)
                sc.activation(n_sb[0:1, :], ps[2][0:1, :], AF.Ln)
                L.inc(sc.activation(n_sb[0:1, :], n_sb[0:1, :], AF.Exp, scale=f32(-1.0)),
                      "p3rec", 1)

        @block.vector
        def _(v, h=h):
            for tq in range(4):
                gq = h * 4 + tq
                v.wait_ge(L.s("p3rec"), gq + 1)
                v.wait_ge(L.s("p3nb"), gq + 1)
                L.inc(v.tensor_copy(nb_sb[:], ps[6][:, :]), "p3nbe", 1)
                for sub in range(2):
                    osl = (2 * gq + sub) % 4
                    uses = (2 * gq + sub) // 4
                    if uses:
                        v.wait_ge(L.s(f"p3sp{osl}"), 16 * uses)
                    L.inc(v.tensor_tensor(otile[osl][:], ps[3 + sub][:, :], nb_sb[:],
                                          ALU.mult), "p3oev", 1)
                    if 2 * gq + sub >= 2:
                        v.wait_ge(L.s("p3stm"), 2 * gq + sub - 1)
                    L.inc(v.tensor_copy(obf[sub][:], otile[osl][:]), "p3cp", 1)
                    L.inc(v.tensor_tensor(sqf[sub][:], obf[sub][:], obf[sub][:],
                                          ALU.mult), "p3sq2", 1)
                v.wait_ge(L.s("p3stm"), 2 * gq + 2)
                if gq == 0:
                    v.wait_ge(L.s("p3z0"), 2)
                v.tensor_tensor(st2_s[0:1, tq * 512:(tq + 1) * 512],
                                st2_s[0:1, tq * 512:(tq + 1) * 512], ps[6][0:1, :], ALU.add)
                L.inc(v.tensor_tensor(st2_q[0:1, tq * 512:(tq + 1) * 512],
                                      st2_q[0:1, tq * 512:(tq + 1) * 512], ps[7][0:1, :],
                                      ALU.add), "p3sta", 1)

        @block.gpsimd
        def _(g, h=h):
            for tq in range(4):
                gq = h * 4 + tq
                for sub in range(2):
                    osl = (2 * gq + sub) % 4
                    cc = 2 * h + sub
                    g.wait_ge(L.s("p3oev"), 2 * gq + sub + 1)
                    g.wait_ge(L.s("p3cp"), 2 * gq + sub + 1)
                    L.inc(g.dma_start(out=aosp[cc, :, tq * 512:(tq + 1) * 512],
                                      in_=otile[osl][:]), f"p3sp{osl}", 16)

    @block.gpsimd
    def _(g):
        for i in range(4):
            if f"p3sp{i}" in L.d:
                g.wait_ge(L.s(f"p3sp{i}"), L.n(f"p3sp{i}"))
        g.wait_ge(L.s("p3sta"), L.n("p3sta"))
    barrier(free=["p3"])
    p3s.__exit__(None, None, None)

    p4s = contextlib.ExitStack()
    p4s.__enter__()
    aob = [sb(f"aob{i}", [128, 512], F32, p4s) for i in range(6)]
    xrb = [sb(f"xrb{i}", [128, 512], F32, p4s) for i in range(6)]
    x2b = [sb(f"x2b{i}", [128, 512], F32, p4s) for i in range(6)]
    P4KD, P4KP, P4SLOT, P4PREV = [0], [0], {}, {}
    mb2 = sb("mb2", [128, 512], F32, p4s)
    rb2 = sb("rb2", [128, 512], F32, p4s)
    x2bf = [sb(f"x2bf{i}", [128, 512], BF16, p4s) for i in range(4)]
    rtmp_t = sb("rtmp_t", [1, T], F32, p4s)
    rtmp = rtmp_t[0:1, :]
    sq2bf = [sb(f"sq2bf{i}", [128, 512], BF16, p4s) for i in range(4)]

    @block.vector
    def _(v):
        L.inc(v.tensor_scalar(rows_m[0:1, :], st2_s[0:1, :], 1.0 / D, None, ALU.mult),
              "r2m", 1)
        v.tensor_scalar(rtmp[0:1, :], st2_q[0:1, :], 1.0 / D, None, ALU.mult)
        v.tensor_tensor(rows_r[0:1, :], rows_m[0:1, :], rows_m[0:1, :], ALU.mult)
        L.inc(v.tensor_tensor(rows_r[0:1, :], rtmp[0:1, :], rows_r[0:1, :], ALU.subtract),
              "p4v2", 1)

    @block.scalar
    def _(sc):
        sc.wait_ge(L.s("p4v2"), 1)
        sc.activation(rows_r[0:1, :], rows_r[0:1, :], AF.Ln, bias=f32(EPS))
        L.inc(sc.activation(rows_r[0:1, :], rows_r[0:1, :], AF.Exp, scale=f32(-0.5)),
              "p4r2", 1)

    _p4last = {}
    for _t4 in range(4):
        for _cc in range(NF):
            _i = P4KD[0] % 6
            _u = P4KD[0] // 6
            P4KD[0] += 1
            P4SLOT[(_t4, _cc)] = (_i, _u)
            P4PREV[(_t4, _cc)] = _p4last.get(_i, -1)
            _p4last[_i] = _t4 * NF + _cc

    def _p4issue_xrb(sc, kk):
        t4b, ccb = divmod(kk, NF)
        i, u = P4SLOT[(t4b, ccb)]
        if u >= 1:
            sc.wait_ge(L.s(f"p4x2{i}"), u)
        L.inc(sc.dma_start(out=xrb[i][:],
                           in_=xT[ccb * 128:(ccb + 1) * 128, t4b * 512:(t4b + 1) * 512]),
              f"p4lB{i}", 16)

    for t4 in range(4):
        @block.tensor
        def _(t, t4=t4):
            t.wait_ge(L.s("p4r2"), 1)
            t.wait_ge(L.s("r2m"), 1)
            if t4 >= 1:
                t.wait_ge(L.s("p4bcev"), t4)
            t.matmul(ps[0][:, :], onesr_f[0:1, :], rows_m[0:1, t4 * 512:(t4 + 1) * 512],
                     start=True, stop=True)
            L.inc(t.matmul(ps[1][:, :], onesr_f[0:1, :], rows_r[0:1, t4 * 512:(t4 + 1) * 512],
                           start=True, stop=True), "p4bc", 1)

        @block.sync
        def _(sync, t4=t4):
            for cc in range(NF):
                i, u = P4SLOT[(t4, cc)]
                if u >= 1:
                    sync.wait_ge(L.s(f"p4x2{i}"), u)
                L.inc(sync.dma_start(out=aob[i][:],
                                     in_=aosp[cc, :, t4 * 512:(t4 + 1) * 512]), f"p4lA{i}", 16)

        @block.vector
        def _(v, t4=t4):
            v.wait_ge(L.s("p4bc"), t4 + 1)
            v.tensor_copy(mb2[:], ps[0][:, :])
            L.inc(v.tensor_copy(rb2[:], ps[1][:, :]), "p4bcev", 1)
            def p4sq(v, j):
                v.wait_ge(L.s("p4cp"), j + 1)
                if j >= 4:
                    v.wait_ge(L.s("p4mm"), 2 * (j - 3))
                L.inc(v.tensor_tensor(sq2bf[j % 4][:], x2bf[j % 4][:], x2bf[j % 4][:],
                                      ALU.mult), "p4sqv", 1)

            for cc in range(NF):
                i, u = P4SLOT[(t4, cc)]
                k = t4 * NF + cc
                v.wait_ge(L.s(f"p4lA{i}"), 16 * (u + 1))
                v.wait_ge(L.s(f"p4lB{i}"), 16 * (u + 1))
                if u >= 1:
                    v.wait_ge(L.s(f"p4sp{i}"), 16 * u)
                    v.wait_ge(L.s("p4cp"), P4PREV[(t4, cc)] + 1)
                v.tensor_tensor(aob[i][:], aob[i][:], mb2[:], ALU.subtract)
                v.tensor_tensor(aob[i][:], aob[i][:], rb2[:], ALU.mult)
                L.inc(v.tensor_tensor(x2b[i][:], aob[i][:], xrb[i][:], ALU.add),
                      f"p4x2{i}", 1)
                if cc >= 3:
                    p4sq(v, k - 3)
            for cc in range(NF - 3, NF):
                p4sq(v, t4 * NF + cc)

        @block.gpsimd
        def _(g, t4=t4):
            for cc in range(NF):
                i, u = P4SLOT[(t4, cc)]
                g.wait_ge(L.s(f"p4x2{i}"), u + 1)
                L.inc(g.dma_start(out=x2sp[cc, :, t4 * 512:(t4 + 1) * 512],
                                  in_=x2b[i][:]), f"p4sp{i}", 16)

        @block.scalar
        def _(sc, t4=t4):
            for cc in range(NF):
                i, u = P4SLOT[(t4, cc)]
                k = t4 * NF + cc
                if k == 0:
                    for kk in range(4):
                        _p4issue_xrb(sc, kk)
                if k + 4 < 4 * NF:
                    _p4issue_xrb(sc, k + 4)
                sc.wait_ge(L.s(f"p4x2{i}"), u + 1)
                if k >= 4:
                    sc.wait_ge(L.s("p4mm"), 2 * (k - 3))
                L.inc(sc.copy(x2bf[k % 4][:], x2b[i][:]), "p4cp", 1)

        @block.tensor
        def _(t, t4=t4):
            for cc in range(NF):
                k = t4 * NF + cc
                t.wait_ge(L.s("p4sqv"), k + 1)
                t.matmul(ps[2][0:1, :], ones_bf[:, 0:1], x2bf[k % 4][:],
                         start=(cc == 0), stop=(cc == NF - 1))
                L.inc(t.matmul(ps[3][0:1, :], ones_bf[:, 0:1], sq2bf[k % 4][:],
                               start=(cc == 0), stop=(cc == NF - 1)), "p4mm", 2)

        @block.vector
        def _(v, t4=t4):
            v.wait_ge(L.s("p4mm"), 2 * NF * (t4 + 1))
            v.tensor_copy(st3_s[0:1, t4 * 512:(t4 + 1) * 512], ps[2][0:1, :])
            L.inc(v.tensor_copy(st3_q[0:1, t4 * 512:(t4 + 1) * 512], ps[3][0:1, :]),
                  "p4st3", 1)

    @block.vector
    def _(v):
        v.wait_ge(L.s("p4st3"), 4)
        L.inc(v.tensor_scalar(rows_m[0:1, :], st3_s[0:1, :], 1.0 / D, None, ALU.mult),
              "r3m", 1)
        v.tensor_scalar(rtmp[0:1, :], st3_q[0:1, :], 1.0 / D, None, ALU.mult)
        v.tensor_tensor(rows_r[0:1, :], rows_m[0:1, :], rows_m[0:1, :], ALU.mult)
        L.inc(v.tensor_tensor(rows_r[0:1, :], rtmp[0:1, :], rows_r[0:1, :], ALU.subtract),
              "p4v3", 1)

    @block.scalar
    def _(sc):
        sc.wait_ge(L.s("p4v3"), 1)
        sc.activation(rows_r[0:1, :], rows_r[0:1, :], AF.Ln, bias=f32(EPS))
        L.inc(sc.activation(rows_r[0:1, :], rows_r[0:1, :], AF.Exp, scale=f32(-0.5)),
              "p4r3", 1)

    @block.gpsimd
    def _(g):
        for i in range(6):
            if f"p4sp{i}" in L.d:
                g.wait_ge(L.s(f"p4sp{i}"), L.n(f"p4sp{i}"))
        g.wait_ge(L.s("p4r3"), 1)
    barrier(free=["p4b", "p4c", "p4l", "p4m", "p4s", "p4v", "p4x", "p4r2"])
    p4s.__exit__(None, None, None)

    p5s = contextlib.ExitStack()
    p5s.__enter__()
    x2c = [sb(f"x2c{i}", [128, 512], F32, p5s) for i in range(6)]
    mb3 = sb("mb3", [128, 512], F32, p5s)
    rb3 = sb("rb3", [128, 512], F32, p5s)
    slab5 = [sb(f"slab5{i}", [128, NF * 512], BF16, p5s) for i in range(2)]
    evf = [sb(f"evf{i}", [128, 512], F32, p5s) for i in range(4)]
    z_sb = sb("z_sb", [32, 512], F32, p5s)
    al_t = sb("al_t", [32, 1536], F32, p5s)
    al_in = al_t[0:4, 0:512]
    al_ex = al_t[0:4, 512:1024]
    al_r = al_t[0:1, 1024:1536]
    al_f = al_t[0:4, 0:512]
    al_ex32 = al_t[0:32, 512:1024]
    al_f32 = al_t[0:32, 0:512]

    @block.vector
    def _(v):
        L.inc(v.memset(al_t[:], 0.0), "p5z0", 1)

    def _d1slot(k):
        return (k % 2) * 3 + (k // 2) % 3

    for H in range(2):
        for tt in range(2):
            t4 = H * 2 + tt
            g5 = t4

            @block.tensor
            def _(t, t4=t4, g5=g5):
                t.wait_ge(L.s("p4r3"), 1)
                t.wait_ge(L.s("r3m"), 1)
                t.wait_ge(L.s("p5ev"), L.n("p5ev"))
                if g5 >= 1:
                    t.wait_ge(L.s("p5bcev"), g5)
                t.matmul(ps[0][:, :], onesr_f[0:1, :], rows_m[0:1, t4 * 512:(t4 + 1) * 512],
                         start=True, stop=True)
                L.inc(t.matmul(ps[1][:, :], onesr_f[0:1, :],
                               rows_r[0:1, t4 * 512:(t4 + 1) * 512],
                               start=True, stop=True), "p5bc", 1)

            @block.sync
            def _(sync, t4=t4, g5=g5):
                for cc in range(0, NF, 2):
                    k = g5 * NF + cc
                    i = _d1slot(k)
                    if k >= 6:
                        sync.wait_ge(L.s(f"p5cx{i}"), k // 6)
                    L.inc(sync.dma_start(out=x2c[i][:],
                                         in_=x2sp[cc, :, t4 * 512:(t4 + 1) * 512]), f"p5lx{i}", 16)

            @block.gpsimd
            def _(g, t4=t4, g5=g5):
                for cc in range(1, NF, 2):
                    k = g5 * NF + cc
                    i = _d1slot(k)
                    if k >= 6:
                        g.wait_ge(L.s(f"p5cx{i}"), k // 6)
                    L.inc(g.dma_start(out=x2c[i][:],
                                      in_=x2sp[cc, :, t4 * 512:(t4 + 1) * 512]), f"p5lx{i}", 16)

            @block.vector
            def _(v, tt=tt, g5=g5):
                v.wait_ge(L.s("p5bc"), g5 + 1)
                v.tensor_copy(mb3[:], ps[0][:, :])
                L.inc(v.tensor_copy(rb3[:], ps[1][:, :]), "p5bcev", 1)
                for cc in range(NF):
                    k = g5 * NF + cc
                    i = _d1slot(k)
                    v.wait_ge(L.s(f"p5lx{i}"), 16 * (k // 6 + 1))
                    v.tensor_tensor(x2c[i][:], x2c[i][:], mb3[:], ALU.subtract)
                    L.inc(v.tensor_tensor(ln[:, cc * TH + tt * 512: cc * TH + (tt + 1) * 512],
                                          x2c[i][:], rb3[:], ALU.mult), f"p5cx{i}", 1)


        for tt in range(2):
            gz = H * 2 + tt

            @block.tensor
            def _(t, tt=tt, gz=gz):
                t.wait_ge(L.s("p5ev"), L.n("p5ev"))
                if gz >= 1:
                    t.wait_ge(L.s("p5zev"), gz)
                for cc in range(NF):
                    k = gz * NF + cc
                    t.wait_ge(L.s(f"p5cx{_d1slot(k)}"), k // 6 + 1)
                    ins = t.matmul(ps[2][0:36, :], ar_slab[:, cc * 36:(cc + 1) * 36],
                                   ln[:, cc * TH + tt * 512: cc * TH + (tt + 1) * 512],
                                   start=(cc == 0), stop=(cc == NF - 1))
                L.inc(ins, "p5z", 1)
                t.wait_ge(L.s("p5ae"), gz + 1)
                t.wait_ge(L.s("p5z0"), 1)
                if gz >= 1:
                    t.wait_ge(L.s("p5ar", ), gz)
                L.inc(t.matmul(ps[3][0:1, :], ones4c_f[:, 0:1], al_ex32,
                               start=True, stop=True), "p5as", 1)
                t.wait_ge(L.s("p5ar"), gz + 1)
                if gz >= 1:
                    t.wait_ge(L.s("p5af"), gz)
                L.inc(t.matmul(ps[4][0:4, :], onesr_f[0:1, 0:4], al_r,
                               start=True, stop=True), "p5ab", 1)
                t.wait_ge(L.s("p5af"), gz + 1)
                if gz >= 1:
                    t.wait_ge(L.s("p5zw"), gz)
                L.inc(t.matmul(ps[5][0:32, :], p4e_c[:, :], al_f32,
                               start=True, stop=True), "p5ax", 1)

            @block.vector
            def _(v, tt=tt, gz=gz, H=H):
                v.wait_ge(L.s("p5z"), gz + 1)
                v.tensor_copy(z_sb[:], ps[2][0:32, :])
                L.inc(v.tensor_scalar(al_in, ps[2][32:36, :], rbsel[:, 0:1],
                                      logp[:, 0:1], ALU.add, ALU.add), "p5zev", 1)
                pass
                v.wait_ge(L.s("p5ab"), gz + 1)
                L.inc(v.tensor_tensor(al_f, al_ex, ps[4][0:4, :], ALU.mult), "p5af", 1)
                v.wait_ge(L.s("p5ax"), gz + 1)
                L.inc(v.tensor_tensor(zw[:, gz * 512:(gz + 1) * 512],
                                      z_sb[:], ps[5][0:32, :], ALU.mult), "p5zw", 1)

            @block.scalar
            def _(sc, gz=gz):
                sc.wait_ge(L.s("p5zev"), gz + 1)
                L.inc(sc.activation(al_ex, al_in, AF.Exp), "p5ae", 1)
                sc.wait_ge(L.s("p5as"), gz + 1)
                sc.activation(al_r, ps[3][0:1, :], AF.Ln)
                L.inc(sc.activation(al_r, al_r, AF.Exp, scale=f32(-1.0)), "p5ar", 1)


        @block.gpsimd
        def _(g, H=H):
            for b in (0, 1):
                gs = H * 8 + b
                if gs >= 2:
                    g.wait_ge(L.s("p5mm"), 2 * (gs - 1))
                L.inc(g.dma_start(out=slab5[gs % 2][:], in_=wbs[b]), f"p5wl{gs % 2}", 16)

        @block.gpsimd
        def _(g):
            for i in range(6):
                if f"p5cx{i}" in L.d:
                    g.wait_ge(L.s(f"p5cx{i}"), L.n(f"p5cx{i}"))
        barrier()

        for b in range(8):
            gs = H * 8 + b
            par = gs % 2

            if b >= 2:
                @block.gpsimd
                def _(g, b=b, gs=gs, par=par):
                    g.wait_ge(L.s("p5mm"), 2 * (gs - 1))
                    L.inc(g.dma_start(out=slab5[par][:], in_=wbs[b]), f"p5wl{par}", 16)

            @block.tensor
            def _(t, b=b, gs=gs, par=par, H=H):
                t.wait_ge(L.s(f"p5wl{par}"), 16 * (gs // 2 + 1))
                for hw in range(2):
                    t.wait_ge(L.s("p5ev"),
                              max(0, L.n("p5ev") - (4 if hw == 0 else 0)))
                    for fi in range(NF):
                        for sub in (2 * hw, 2 * hw + 1):
                            lhsT = slab5[par][:, fi * 512 + sub * 128: fi * 512 + sub * 128 + 128]
                            for tt in range(2):
                                t.matmul(ps[sub * 2 + tt][:, :], lhsT,
                                         ln[:, fi * TH + tt * 512: fi * TH + (tt + 1) * 512],
                                         start=(fi == 0), stop=False)
                    t.wait_ge(L.s("p5zw"), 2 * (H + 1))
                    for sub in (2 * hw, 2 * hw + 1):
                        for tt in range(2):
                            ins = t.matmul(ps[sub * 2 + tt][:, :],
                                           bsel[:, b * 512 + sub * 128: b * 512 + sub * 128 + 128],
                                           zw[:, (H * 2 + tt) * 512:(H * 2 + tt + 1) * 512],
                                           start=False, stop=True)
                    L.inc(ins, "p5mm", 1)

            @block.vector
            def _(v, b=b, gs=gs):
                for sub in range(4):
                    for tt in range(2):
                        if tt == 0 and sub % 2 == 0:
                            v.wait_ge(L.s("p5mm"), 2 * gs + sub // 2 + 1)
                        i = sub * 2 + tt
                        cc = b * 4 + sub
                        gt_ = gs * 8 + i
                        sl = gt_ % 4
                        if gt_ >= 4:
                            v.wait_ge(L.s(f"p5sp{sl}"), 16 * (gt_ // 4))
                        L.inc(v.tensor_scalar(evf[sl][:], ps[i][:, :], bbc[:, cc:cc + 1],
                                              None, ALU.add), "p5ev", 1)

            @block.scalar
            def _(sc, b=b, gs=gs, H=H):
                base = L.n("p5ev") - 8
                for sub in range(4):
                    for tt in range(2):
                        i = sub * 2 + tt
                        cc = b * 4 + sub
                        t4 = H * 2 + tt
                        gt_ = gs * 8 + i
                        sl = gt_ % 4
                        sc.wait_ge(L.s("p5ev"), base + i + 1)
                        L.inc(sc.dma_start(out=ffsp[cc, :, t4 * 512:(t4 + 1) * 512],
                                           in_=evf[sl][:]), f"p5sp{sl}", 16)

    @block.gpsimd
    def _(g):
        for i in range(4):
            if f"p5sp{i}" in L.d:
                g.wait_ge(L.s(f"p5sp{i}"), L.n(f"p5sp{i}"))
    barrier()
    p5s.__exit__(None, None, None)

    p6s = contextlib.ExitStack()
    p6s.__enter__()
    ffb = [sb(f"ffb{i}", [128, 512], F32, p6s) for i in range(6)]
    x2r = [sb(f"x2r{i}", [128, 512], F32, p6s) for i in range(6)]
    ob6 = [sb(f"ob6{i}", [128, 512], F32, p6s) for i in range(6)]
    P6KD, P6KP, P6SLOT = [0], [0], {}
    mb4 = sb("mb4", [128, 512], F32, p6s)
    rb4 = sb("rb4", [128, 512], F32, p6s)
    fbf = [sb(f"fbf{i}", [128, 512], BF16, p6s) for i in range(6)]
    sq4 = [sb(f"sq4{i}", [128, 512], BF16, p6s) for i in range(6)]
    rtmp_t6 = sb("rtmp_t6", [1, T], F32, p6s)
    rtmp6 = rtmp_t6[0:1, :]

    for t4 in range(4):
        @block.sync
        def _(sync, t4=t4):
            for cc in range(NF):
                k = t4 * NF + cc
                if k % 2 != 0:
                    continue
                i = _d1slot(k)
                if k >= 6:
                    sync.wait_ge(L.s("p6cp"), k - 5)
                L.inc(sync.dma_start(out=ffb[i][:],
                                     in_=ffsp[cc, :, t4 * 512:(t4 + 1) * 512]), f"p6lf{i}", 16)

        @block.gpsimd
        def _(g, t4=t4):
            for cc in range(NF):
                k = t4 * NF + cc
                if k % 2 != 1:
                    continue
                i = _d1slot(k)
                if k >= 6:
                    g.wait_ge(L.s("p6cp"), k - 5)
                L.inc(g.dma_start(out=ffb[i][:],
                                  in_=ffsp[cc, :, t4 * 512:(t4 + 1) * 512]), f"p6lf{i}", 16)

        @block.scalar
        def _(sc, t4=t4):
            for cc in range(NF):
                k = t4 * NF + cc
                i = _d1slot(k)
                sc.wait_ge(L.s(f"p6lf{i}"), 16 * (k // 6 + 1))
                if k >= 6:
                    sc.wait_ge(L.s("p6mm"), 2 * (k - 5))
                L.inc(sc.copy(fbf[k % 6][:], ffb[i][:]), "p6cp", 1)

        @block.vector
        def _(v, t4=t4):
            for cc in range(NF):
                k = t4 * NF + cc
                v.wait_ge(L.s("p6cp"), k + 1)
                if k >= 6:
                    v.wait_ge(L.s("p6mm"), 2 * (k - 5))
                L.inc(v.tensor_tensor(sq4[k % 6][:], fbf[k % 6][:], fbf[k % 6][:],
                                      ALU.mult), "p6sqv", 1)

        @block.tensor
        def _(t, t4=t4):
            for cc in range(NF):
                k = t4 * NF + cc
                t.wait_ge(L.s("p6sqv"), k + 1)
                t.matmul(ps[0][0:1, :], ones_bf[:, 0:1], fbf[k % 6][:],
                         start=(cc == 0), stop=(cc == NF - 1))
                L.inc(t.matmul(ps[1][0:1, :], ones_bf[:, 0:1], sq4[k % 6][:],
                               start=(cc == 0), stop=(cc == NF - 1)), "p6mm", 2)

        @block.vector
        def _(v, t4=t4):
            v.wait_ge(L.s("p6mm"), 2 * NF * (t4 + 1))
            v.tensor_copy(st3_s[0:1, t4 * 512:(t4 + 1) * 512], ps[0][0:1, :])
            L.inc(v.tensor_copy(st3_q[0:1, t4 * 512:(t4 + 1) * 512], ps[1][0:1, :]),
                  "p6st", 1)

    @block.vector
    def _(v):
        v.wait_ge(L.s("p6st"), 4)
        L.inc(v.tensor_scalar(rows_m[0:1, :], st3_s[0:1, :], 1.0 / D, None, ALU.mult),
              "r4m", 1)
        v.tensor_scalar(rtmp6[0:1, :], st3_q[0:1, :], 1.0 / D, None, ALU.mult)
        v.tensor_tensor(rows_r[0:1, :], rows_m[0:1, :], rows_m[0:1, :], ALU.mult)
        L.inc(v.tensor_tensor(rows_r[0:1, :], rtmp6[0:1, :], rows_r[0:1, :], ALU.subtract),
              "p6v4", 1)

    @block.scalar
    def _(sc):
        sc.wait_ge(L.s("p6v4"), 1)
        sc.activation(rows_r[0:1, :], rows_r[0:1, :], AF.Ln, bias=f32(EPS))
        L.inc(sc.activation(rows_r[0:1, :], rows_r[0:1, :], AF.Exp, scale=f32(-0.5)),
              "p6r4", 1)

    for _t4 in range(4):
        for _cc in range(NF):
            _i = P6KD[0] % 6
            _u = P6KD[0] // 6
            P6KD[0] += 1
            P6SLOT[(_t4, _cc)] = (_i, _u)

    def _p6issue_x2r(g, kk):
        t4b, ccb = divmod(kk, NF)
        i, u = P6SLOT[(t4b, ccb)]
        if u >= 1:
            g.wait_ge(L.s(f"p6ou{i}"), u)
        L.inc(g.dma_start(out=x2r[i][:],
                          in_=x2sp[ccb, :, t4b * 512:(t4b + 1) * 512]), f"p6lx{i}", 16)

    for t4 in range(4):
        @block.tensor
        def _(t, t4=t4):
            t.wait_ge(L.s("p6r4"), 1)
            t.wait_ge(L.s("r4m"), 1)
            if t4 >= 1:
                t.wait_ge(L.s("p6bcev"), t4)
            t.matmul(ps[2][:, :], onesr_f[0:1, :], rows_m[0:1, t4 * 512:(t4 + 1) * 512],
                     start=True, stop=True)
            L.inc(t.matmul(ps[3][:, :], onesr_f[0:1, :], rows_r[0:1, t4 * 512:(t4 + 1) * 512],
                           start=True, stop=True), "p6bc", 1)

        @block.sync
        def _(sync, t4=t4):
            if t4 == 0:
                sync.wait_ge(L.s("p6cp"), 128)
            for cc in range(NF):
                i, u = P6SLOT[(t4, cc)]
                if u >= 1:
                    sync.wait_ge(L.s(f"p6ou{i}"), u)
                L.inc(sync.dma_start(out=ffb[i][:],
                                     in_=ffsp[cc, :, t4 * 512:(t4 + 1) * 512]), f"p6lf2{i}", 16)

        @block.vector
        def _(v, t4=t4):
            v.wait_ge(L.s("p6bc"), t4 + 1)
            v.tensor_copy(mb4[:], ps[2][:, :])
            L.inc(v.tensor_copy(rb4[:], ps[3][:, :]), "p6bcev", 1)
            for cc in range(NF):
                i, u = P6SLOT[(t4, cc)]
                v.wait_ge(L.s(f"p6lf2{i}"), 16 * (u + 1))
                v.wait_ge(L.s(f"p6lx{i}"), 16 * (u + 1))
                if u >= 1:
                    v.wait_ge(L.s(f"p6o{i}"), 16 * u)
                v.tensor_tensor(ffb[i][:], ffb[i][:], mb4[:], ALU.subtract)
                v.tensor_tensor(ffb[i][:], ffb[i][:], rb4[:], ALU.mult)
                L.inc(v.tensor_tensor(ob6[i][:], ffb[i][:], x2r[i][:], ALU.add),
                      f"p6ou{i}", 1)

        @block.gpsimd
        def _(g, t4=t4):
            for cc in range(NF):
                k = t4 * NF + cc
                if k == 0:
                    for kk in range(6):
                        _p6issue_x2r(g, kk)
                if k + 6 < 4 * NF:
                    _p6issue_x2r(g, k + 6)

        @block.scalar
        def _(sc, t4=t4):
            for cc in range(NF):
                i, u = P6SLOT[(t4, cc)]
                sc.wait_ge(L.s(f"p6ou{i}"), u + 1)
                L.inc(sc.dma_start(out=outT[cc, :, t4 * 512:(t4 + 1) * 512],
                                   in_=ob6[i][:]), f"p6o{i}", 16)

    @block.gpsimd
    def _(g):
        for i in range(6):
            if f"p6o{i}" in L.d:
                g.wait_ge(L.s(f"p6o{i}"), L.n(f"p6o{i}"))
    barrier()
    if DEBUG_SPILLS:
        @block.sync
        def _(sync):
            L.inc(sync.dma_start(out=dbg_zw, in_=zw[:]), "dbgo", 16)
            L.inc(sync.dma_start(out=dbg_ln, in_=ln[:]), "dbgo", 16)
            L.inc(sync.dma_start(out=dbg_bsel, in_=bsel[:]), "dbgo", 16)
            L.inc(sync.dma_start(out=dbg_ar, in_=ar_slab[:]), "dbgo", 16)
            sync.wait_ge(L.s("dbgo"), L.n("dbgo"))
        barrier()
    p6s.__exit__(None, None, None)

    es.__exit__(None, None, None)
    return nc


def _prep_shared(inputs):
    c = _host_consts()
    sh = {
        "wq": _tile_w(np.ascontiguousarray(inputs["L0_attn_Wq"].T)),
        "wk": _tile_w(np.ascontiguousarray(inputs["L0_attn_Wk"].T)),
        "wv": _tile_w(np.ascontiguousarray(inputs["L0_attn_Wv"].T)),
        "wbs": _tile_w(np.ascontiguousarray(inputs["L0_ffn_W_base"].T)),
        "bq_t": np.ascontiguousarray(inputs["L0_attn_bq"].reshape(NF, 128).T),
        "bk_t": np.ascontiguousarray(inputs["L0_attn_bk"].reshape(NF, 128).T),
        "bb_t": np.ascontiguousarray(inputs["L0_ffn_b_base"].reshape(NF, 128).T),
        "bv_r": np.ascontiguousarray(inputs["L0_attn_bv"].reshape(1, D)),
        "bwT": np.ascontiguousarray(inputs["L0_attn_bias_weight"].T),
        "active": np.ascontiguousarray(inputs["L0_ffn_active"].reshape(1, 4)),
        "prior": np.ascontiguousarray(
            np.concatenate([inputs["L0_ffn_domain_prior"].reshape(16, 1),
                            np.zeros((16, 1), np.float32)])),
        "rb_in": np.ascontiguousarray(
            np.concatenate([inputs["L0_ffn_router_b"].reshape(16, 1),
                            np.zeros((16, 1), np.float32)])),
        "routW": np.ascontiguousarray(
            np.concatenate([inputs["L0_ffn_router_W"],
                            np.zeros((16, D), np.float32)])),
        "loraA": np.ascontiguousarray(inputs["L0_ffn_lora_A"].reshape(128, D)),
        "loraB": np.ascontiguousarray(inputs["L0_ffn_lora_B"].transpose(0, 2, 1).reshape(128, D)),
    }
    sh.update(c)
    return sh


def kernel(**inputs):
    x = inputs["L0_attn_x"]
    sh = _prep_shared(inputs)
    nc = build_program()
    in_maps = []
    for core in range(N_CORES):
        m = dict(sh)
        m["xT"] = np.ascontiguousarray(x[core].T)
        in_maps.append(m)
    res = run_bass_kernel_spmd(nc, in_maps, core_ids=list(range(N_CORES)))
    out = np.empty((N_CORES, T, D), dtype=np.float32)
    for core in range(N_CORES):
        o = res.results[core]["outT"]
        out[core] = np.asarray(o).reshape(D, T).T
    return out
